# version 14
# baseline (speedup 1.0000x reference)
import numpy as np
from contextlib import ExitStack

import concourse.bass as bass
import concourse.mybir as mybir
from concourse.bass_utils import run_bass_kernel_spmd

F32 = mybir.dt.float32
BF16 = mybir.dt.bfloat16
AF = mybir.ActivationFunctionType
ALU = mybir.AluOpType
AX = mybir.AxisListType

D = 1024
KC = 8
T = 4096
NH = 16
DKH = 64
HID0 = 2816
HC0 = 22
NEXP = 7
HID1 = 4096
HC1 = 32
TQ = 1024
NQ = T // TQ
LN_EPS = 1e-5
ATT_SCALE = 0.125

ENGINES = ("tensor", "vector", "scalar", "gpsimd", "sync")
_UNIQ = [0]


def _tag():
    _UNIQ[0] += 1
    return f"_u{_UNIQ[0]}"


class Sem:
    def __init__(self, nc, stack, name):
        self.h = stack.enter_context(nc.semaphore(name))
        self.n = 0

    def reserve(self, k):
        self.n += k
        return self.n


class Prog:
    def __init__(self, nc, stack):
        self.nc = nc
        self.streams = {k: [] for k in ENGINES}
        self.s = {k: Sem(nc, stack, "s_" + k) for k in
                  ("pe", "act", "dve", "pool", "dw", "dx", "dq")}

    def on(self, eng, fn):
        self.streams[eng].append(fn)

    def op(self, eng, sem, waits, fn):
        tgt = sem.reserve(1)

        def body(e, fn=fn, waits=tuple(waits)):
            for ws, wc in waits:
                if wc > 0:
                    e.wait_ge(ws.h, wc)
            fn(e).then_inc(sem.h, 1)
        self.on(eng, body)
        return tgt

    def wait(self, eng, sem, count):
        if count > 0:
            self.on(eng, lambda e, c=count: e.wait_ge(sem.h, c))

    def dma(self, queue, sem, dst, src, waits=(), **kw):
        tgt = sem.reserve(16)

        def body(e, d=dst, s=src, w=tuple(waits), kw=kw):
            for ws, wc in w:
                if wc > 0:
                    e.wait_ge(ws.h, wc)
            e.dma_start(d, s, **kw).then_inc(sem.h, 16)
        self.on(queue, body)
        return tgt

    def flush(self, name=None):
        live = {k: v for k, v in self.streams.items() if v}
        self.streams = {k: [] for k in ENGINES}
        if not live:
            return
        with self.nc.Block(name) as blk:
            for eng_name, fns in live.items():
                def body(eng, fns=fns):
                    for f in fns:
                        f(eng)
                getattr(blk, eng_name)(body)


class WCache:

    def __init__(self, nc, name, total_cols):
        self.t = nc.dram_tensor(name, [128, total_cols], BF16, kind="Internal").ap()
        self.off = {}
        self.built = {}
        self.cur = 0
        self.total = total_cols

    def load(self, P, key, dst3d, dst2d, src3d, cols, waits):
        if key in self.built:
            off = self.off[key]
            c = P.dma("gpsimd", P.s["dw"], dst2d, self.t[:, off:off + cols],
                      waits=waits)
            return (P.s["dw"], c), None
        off = self.cur
        assert off + cols <= self.total, (key, off, cols, self.total)
        self.off[key] = off
        self.cur += cols
        c1 = P.dma("gpsimd", P.s["dw"], dst3d, src3d, waits=waits)
        c2 = P.dma("sync", P.s["dx"], self.t[:, off:off + cols], dst2d,
                   waits=[(P.s["dw"], c1)])
        self.built[key] = c2
        return (P.s["dw"], c1), (P.s["dx"], c2)


def fm3(ap2d, p=128):
    return ap2d.rearrange("(c p) t -> p c t", p=p)


class Consts:
    def __init__(self, nc, P, stack, need_attn=True):
        sb = lambda n, shp, dt: stack.enter_context(nc.sbuf_tensor(n, shp, dt))
        self.ident = sb("c_ident", [128, 128], BF16)
        self.ones_col = sb("c_ones_col", [128, 1], BF16)
        self.ones_row = sb("c_ones_row", [1, 128], BF16)
        self.ones_row32 = sb("c_ones_row32", [1, 128], F32)
        iota = sb("c_iota", [128, 128], F32)
        self.need_attn = need_attn
        if need_attn:
            self.dmask2 = [sb(f"c_dmask2_{i}", [128, 1024], BF16) for i in range(2)]
            self.diag_mask = [self.dmask2[r // 2][:, (r % 2) * 512:(r % 2 + 1) * 512]
                              for r in range(4)]
            iota_q = sb("c_iota_q", [128, 512], F32)

        def g(eng):
            eng.iota(iota[:], [[-1, 128]], base=0, channel_multiplier=1,
                     allow_small_or_imprecise_dtypes=True)
            eng.memset(self.ones_col[:], 1.0)
            eng.memset(self.ones_row[:], 1.0)
            eng.memset(self.ones_row32[:], 1.0)
            if need_attn:
                eng.iota(iota_q[:], [[1, 512]], base=0, channel_multiplier=-1,
                         allow_small_or_imprecise_dtypes=True)
        P.on("gpsimd", g)
        P.flush("consts_a")

        def v(eng):
            eng.tensor_scalar(self.ident[:], iota[:], 0.0, None, ALU.is_equal)
            if need_attn:
                for r in range(4):
                    eng.tensor_scalar(self.diag_mask[r][:], iota_q[:],
                                      float(128 * r), None, ALU.is_ge)
        P.on("vector", v)
        P.flush("consts_b")


def emit_ln(P, C, lnp, x_q, xn_q, tq, x_wait, pe_stat_waits=(), pe_bc_waits=(),
            copy_waits=(), pre_cast=False):
    S = P.s
    if pre_cast:
        c_copy = P.op("vector", S["dve"], [x_wait] + list(copy_waits),
                      lambda e: e.tensor_tensor(lnp["sq"][:], xn_q[:], xn_q[:],
                                                ALU.mult))
    else:
        c_copy = P.op("vector", S["dve"], [x_wait] + list(copy_waits), lambda e: (
            e.tensor_copy(xn_q[:], x_q[:]),
            e.tensor_tensor(lnp["sq"][:], xn_q[:], xn_q[:], ALU.mult))[-1])
    c_cb = 0
    c_done = 0
    c_bc = 0
    for it in range(tq // 512):
        t0, t1 = it * 512, (it + 1) * 512
        r0, r1 = 0, 512

        def stats_mms(e, t0=t0, t1=t1):
            for c in range(KC):
                last = e.matmul(lnp["s1ps"], C.ones_col[:],
                                xn_q[:, c * tq + t0: c * tq + t1],
                                start=(c == 0), stop=(c == KC - 1))
            for c in range(KC):
                last = e.matmul(lnp["s2ps"], C.ones_col[:],
                                lnp["sq"][:, c * tq + t0: c * tq + t1],
                                start=(c == 0), stop=(c == KC - 1))
            return last
        w = [(S["dve"], c_copy)] + (list(pe_stat_waits) if it == 0 else [])
        if it > 0:
            w.append((S["dve"], c_var))
        c_mm = P.op("tensor", S["pe"], w, stats_mms)

        def dve_stats(e, r0=r0, r1=r1):
            e.tensor_scalar(lnp["m32"][0:1, r0:r1], lnp["s1ps"], 1.0 / D, None, ALU.mult)
            e.tensor_tensor(lnp["var"][0:1, r0:r1], lnp["m32"][0:1, r0:r1],
                            lnp["m32"][0:1, r0:r1], ALU.mult)
            e.scalar_tensor_tensor(lnp["var"][0:1, r0:r1], lnp["s2ps"], 1.0 / D,
                                   lnp["var"][0:1, r0:r1], ALU.mult, ALU.subtract)
            return e.tensor_scalar(lnp["var"][0:1, r0:r1], lnp["var"][0:1, r0:r1],
                                   LN_EPS, None, ALU.add)
        c_var = P.op("vector", S["dve"], [(S["pe"], c_mm)], dve_stats)

        c_sd = P.op("scalar", S["act"], [(S["dve"], c_var)],
                    lambda e, r0=r0, r1=r1: e.activation(
                        lnp["sd"][0:1, r0:r1], lnp["var"][0:1, r0:r1], AF.Sqrt))

        def dve_rows(e, r0=r0, r1=r1):
            e.tensor_scalar(lnp["nm_bf"][0:1, r0:r1], lnp["m32"][0:1, r0:r1],
                            -1.0, None, ALU.mult)
            e.reciprocal(lnp["r32"][0:1, r0:r1], lnp["sd"][0:1, r0:r1])
            return e.tensor_scalar(lnp["r_bf"][0:1, r0:r1], lnp["r32"][0:1, r0:r1],
                                   1.0, None, ALU.mult)
        c_rows = P.op("vector", S["dve"], [(S["act"], c_sd), (S["pe"], c_bc)],
                      dve_rows)

        def bcast_mms(e):
            e.matmul(lnp["mbps"], C.ones_row[:], lnp["nm_bf"][0:1, 0:512],
                     start=True, stop=True)
            return e.matmul(lnp["rbps"], C.ones_row[:], lnp["r_bf"][0:1, 0:512],
                            start=True, stop=True)
        w = [(S["dve"], c_rows)]
        if it == 0:
            w += list(pe_bc_waits)
        else:
            w.append((S["act"], c_cb))
        c_bc = P.op("tensor", S["pe"], w, bcast_mms)

        def act_cb(e):
            e.activation(lnp["mb"][:, 0:512], lnp["mbps"], AF.Copy)
            return e.activation(lnp["rb"][:, 0:512], lnp["rbps"], AF.Copy)
        c_cb = P.op("scalar", S["act"], [(S["pe"], c_bc)], act_cb)

        def dve_norm(e, t0=t0, t1=t1):
            for c in range(KC):
                sl = slice(c * tq + t0, c * tq + t1)
                e.tensor_tensor(xn_q[:, sl], xn_q[:, sl], lnp["mb"][:, 0:512], ALU.add)
                last = e.tensor_tensor(xn_q[:, sl], xn_q[:, sl], lnp["rb"][:, 0:512], ALU.mult)
            return last
        c_done = P.op("vector", S["dve"], [(S["act"], c_cb)], dve_norm)
    return {"xn": c_done, "act_bc": c_cb, "var": c_var, "copy": c_copy}


def make_lnp(nc, stack, tag, tq, sq_alias, pp, mb_alias=None, rb_alias=None,
             nm_alias=None, r_alias=None, m32_alias=None, var_alias=None):
    sb = lambda n, shp, dt: stack.enter_context(nc.sbuf_tensor(n + tag, shp, dt))
    d = {
        "sq": sq_alias,
        "m32": m32_alias if m32_alias is not None else sb("ln_m32", [1, 512], F32),
        "var": var_alias if var_alias is not None else sb("ln_var", [1, 512], F32),
        "nm_bf": nm_alias if nm_alias is not None else sb("ln_nm", [1, 512], BF16),
        "r_bf": r_alias if r_alias is not None else sb("ln_rbf", [1, 512], BF16),
        "mb": mb_alias if mb_alias is not None else sb("ln_mb", [128, 512], BF16),
        "rb": rb_alias if rb_alias is not None else sb("ln_rb", [128, 512], BF16),
        "s1ps": pp["s1"], "s2ps": pp["s2"],
        "mbps": pp["mb"], "rbps": pp["rb"],
    }
    d["sd"] = d["var"]
    d["r32"] = d["m32"]
    return d


def emit_ffn0(nc, P, C, stack, xin, xout, W, tail, dbg=None):
    S = P.s
    ut = _tag()
    sb = lambda n, shp, dt: stack.enter_context(nc.sbuf_tensor(n + ut, shp, dt))
    x_q = sb("f0_x", [128, KC * TQ], F32)
    xn_q = sb("f0_xn", [128, KC * TQ], BF16)
    g_sb = sb("f0_g", [128, HC0 * TQ], BF16)
    wg_buf = [sb(f"f0_wg{i}", [128, KC * 512], BF16) for i in range(2)]
    wu_buf = [sb(f"f0_wu{i}", [128, KC * 512], BF16) for i in range(2)]
    wd_buf = [sb(f"f0_wd{i}", [128, HC0 * 256], BF16) for i in range(2)]
    t1_buf = [sb(f"f0_t1{i}", [128, 512], BF16) for i in range(2)]
    td_buf = [sb(f"f0_td{i}", [128, 512], BF16) for i in range(2)]
    wrows = sb("f0_wrows", [1, NEXP * TQ], BF16)
    wb_sb = sb("f0_wb", [128, TQ], BF16)
    wr_sb = sb("f0_wr", [128, KC * NEXP], BF16)
    rbb_sb = sb("f0_rbb", [128, NEXP], F32)
    rb_row = sb("f0_rbrow", [1, NEXP], F32)
    rt = {n: sb("f0_" + n, [128, NEXP], F32)
          for n in ("lg", "probs", "ismax", "probs2", "keep", "wun")}
    wfin = sb("f0_wfin", [128, NEXP], BF16)
    rt1 = {n: sb("f0_" + n, [128, 1], F32) for n in ("m1", "m2", "den", "invd")}

    ps = lambda n, shp, dt: stack.enter_context(nc.psum_tensor(n + ut, shp, dt))
    pg = [ps("f0_pg0", [128, 512], F32), ps("f0_pg1", [128, 512], F32)]
    pu = [ps("f0_pu0", [128, 512], F32), ps("f0_pu1", [128, 512], F32)]
    pd = [ps("f0_pd0", [128, 512], F32), ps("f0_pd1", [128, 512], F32)]
    pmisc = ps("f0_pm", [128, 512], F32)

    lnp = make_lnp(nc, stack, "f0" + ut, TQ, g_sb[:, 0:KC * TQ],
                   {"s1": pd[0][0:1, 0:512], "s2": pd[1][0:1, 0:512],
                    "mb": pg[0][:, 0:512], "rb": pg[1][:, 0:512]})

    c_wr = P.dma("gpsimd", S["dw"], wr_sb[:].rearrange("p (c o) -> p c o", o=NEXP),
                 W["wr"].rearrange("(c p) o -> p c o", p=128))
    c_rb = P.dma("sync", S["dx"], rb_row[0:1, :],
                 W["rb"].rearrange("(o e) -> o e", o=1))
    c_rbbmm = P.op("tensor", S["pe"], [(S["dx"], c_rb)],
                   lambda e: e.matmul(pmisc[:, 0:NEXP], C.ones_row32[:],
                                      rb_row[0:1, :], start=True, stop=True))
    c_rbb = P.op("vector", S["dve"], [(S["pe"], c_rbbmm)],
                 lambda e: e.tensor_copy(rbb_sb[:], pmisc[:, 0:NEXP]))

    wc = W["wc"]
    tag = W["tag"]
    slot_x_g = [[], []]; slot_x_u = [[], []]; slot_x_d = [[], []]
    cons_wg = [0, 0]; cons_wu = [0, 0]; cons_wd = [0, 0]
    c_q_done = 0
    c_last_silu = 0
    c_last_gmul = 0
    c_pmisc_free = c_rbb
    c_pmisc_free_sem = "dve"
    c_wb_store = 0
    last_evac = 0
    dbg_pend = {"xn": 0, "w": 0, "g": 0}
    CH = [(i * 4, min(4, HC0 - i * 4)) for i in range((HC0 + 3) // 4)]

    for q in range(NQ):
        qsl = slice(q * TQ, (q + 1) * TQ)
        c_x = P.dma("sync", S["dx"], x_q[:].rearrange("p (c t) -> p c t", t=TQ),
                    fm3(xin)[:, :, qsl],
                    waits=[(S["dve"], c_q_done), (S["dx"], c_wb_store)])

        ln = emit_ln(P, C, lnp, x_q, xn_q, TQ, (S["dx"], c_x),
                     pe_stat_waits=[(S["dve"], c_q_done)],
                     pe_bc_waits=[(S["act"], c_last_silu)],
                     copy_waits=[(S["dq"], dbg_pend["xn"])])
        c_xn = ln["xn"]

        c_lg_last = 0
        c_rtcopy = 0
        c_tr = 0
        for qt in range(TQ // 128):
            def route_mm(e, qt=qt):
                for c in range(KC):
                    last = e.matmul(
                        pmisc[:, 0:NEXP],
                        xn_q[:, c * TQ + qt * 128: c * TQ + (qt + 1) * 128],
                        wr_sb[:, c * NEXP:(c + 1) * NEXP],
                        start=(c == 0), stop=(c == KC - 1))
                return last
            w = [(S["dve"], c_xn), (S["dw"], c_wr)]
            w.append((S[c_pmisc_free_sem], c_pmisc_free))
            if dbg is not None and "rt" in dbg:
                w.append((S["dq"], dbg_pend["w"]))
            c_lmm = P.op("tensor", S["pe"], w, route_mm)

            c_lg_last = P.op("vector", S["dve"], [(S["pe"], c_lmm), (S["dve"], c_rbb)],
                             lambda e: e.tensor_tensor(rt["lg"][:], pmisc[:, 0:NEXP],
                                                       rbb_sb[:], ALU.mult))
            c_pmisc_free, c_pmisc_free_sem = c_lg_last, "dve"

            c_sig = P.op("scalar", S["act"], [(S["dve"], c_lg_last)],
                         lambda e: e.activation(rt["probs"][:], rt["lg"][:], AF.Sigmoid))

            cc = P.op("vector", S["dve"], [(S["act"], c_sig), (S["pe"], c_tr)],
                      lambda e: e.tensor_reduce(rt1["m1"][:], rt["probs"][:],
                                                AX.X, ALU.max))
            cc = P.op("vector", S["dve"], [(S["dve"], cc)],
                      lambda e: e.tensor_scalar(rt["ismax"][:], rt["probs"][:],
                                                rt1["m1"][:, 0:1], None, ALU.is_ge))
            cc = P.op("vector", S["dve"], [(S["dve"], cc)],
                      lambda e: e.scalar_tensor_tensor(rt["probs2"][:], rt["ismax"][:],
                                                       -2.0, rt["probs"][:],
                                                       ALU.mult, ALU.add))
            cc = P.op("vector", S["dve"], [(S["dve"], cc)],
                      lambda e: e.tensor_reduce(rt1["m2"][:], rt["probs2"][:],
                                                AX.X, ALU.max))
            cc = P.op("vector", S["dve"], [(S["dve"], cc)],
                      lambda e: (e.tensor_scalar(rt["keep"][:], rt["probs"][:],
                                                 rt1["m2"][:, 0:1], None, ALU.is_ge),
                                 e.tensor_tensor(rt1["den"][:], rt1["m1"][:],
                                                 rt1["m2"][:], ALU.add))[-1])
            cc = P.op("vector", S["dve"], [(S["dve"], cc)],
                      lambda e: (e.tensor_tensor(rt["wun"][:], rt["probs"][:],
                                                 rt["keep"][:], ALU.mult),
                                 e.reciprocal(rt1["invd"][:], rt1["den"][:]))[-1])
            c_w = P.op("vector", S["dve"], [(S["dve"], cc)],
                       lambda e: e.tensor_scalar(wfin[:], rt["wun"][:],
                                                 rt1["invd"][:, 0:1], 0.5,
                                                 ALU.mult, ALU.mult))
            if dbg is not None and "rt" in dbg and q == 0 and qt == 0:
                for nm, t in [("lg", rt["lg"]), ("probs", rt["probs"]),
                              ("rbb", rbb_sb), ("keep", rt["keep"]),
                              ("den", rt1["den"])]:
                    dbg_pend["w"] = P.dma("scalar", S["dq"], dbg["rt_" + nm],
                                          t[:, :], waits=[(S["dve"], c_w)])
                    tail.append((S["dq"], dbg_pend["w"]))
                dbg_pend["w"] = P.dma("scalar", S["dq"], dbg["rt_wfin"],
                                      wfin[:, :], waits=[(S["dve"], c_w)])
                tail.append((S["dq"], dbg_pend["w"]))

            c_rtcopy_grp = 0
            for g0, g1 in ((0, 4), (4, NEXP)):
                def row_mms(e, g0=g0, g1=g1):
                    for ex in range(g0, g1):
                        last = e.matmul(pmisc[0:1, (ex - g0) * 128:(ex - g0 + 1) * 128],
                                        wfin[:, ex:ex + 1], C.ident[:],
                                        start=True, stop=True)
                    return last
                wts = [(S["dve"], c_w)]
                if g0 == 0:
                    wts += [(S[c_pmisc_free_sem], c_pmisc_free),
                            (S["act"], c_rtcopy)]
                else:
                    wts.append((S["act"], c_rtcopy_grp))
                c_tr = P.op("tensor", S["pe"], wts, row_mms)

                def row_copy(e, g0=g0, g1=g1, qt=qt):
                    dst = wrows[0:1, :].rearrange("o (x t) -> o x t", t=TQ)[
                        :, g0:g1, qt * 128: (qt + 1) * 128]
                    src = pmisc[0:1, 0:(g1 - g0) * 128].rearrange(
                        "o (x t) -> o x t", t=128)
                    return e.activation(dst, src, AF.Copy)
                wts = [(S["pe"], c_tr)]
                if g0 == 0 and qt == 0:
                    wts.append((S["dq"], dbg_pend["w"]))
                c_rtcopy_grp = P.op("scalar", S["act"], wts, row_copy)
            c_rtcopy = c_rtcopy_grp
            c_pmisc_free, c_pmisc_free_sem = c_rtcopy, "act"
        c_routing = c_rtcopy
        if dbg is not None:
            dbg_pend["xn"] = P.dma(
                "scalar", S["dq"], fm3(dbg["xn"])[:, :, qsl],
                xn_q[:].rearrange("p (c t) -> p c t", t=TQ),
                waits=[(S["dve"], c_xn)])
            tail.append((S["dq"], dbg_pend["xn"]))
            dbg_pend["w"] = P.dma(
                "scalar", S["dq"],
                dbg["w"].rearrange("x (q t) -> x q t", t=TQ)[:, q:q + 1, :].rearrange("x q t -> q x t"),
                wrows[0:1, :].rearrange("o (x t) -> o x t", t=TQ),
                waits=[(S["act"], c_routing)])
            tail.append((S["dq"], dbg_pend["w"]))

        for ei in range(NEXP + 1):
            shared = (ei == 0)
            wg_ap = W["sg"] if shared else W["rg"][ei - 1]
            wu_ap = W["su"] if shared else W["ru"][ei - 1]
            wd_ap = W["sd"] if shared else W["rd"][ei - 1]

            c_wb_ready = 0
            if not shared:
                for tt in range(2):
                    def wb_mm(e, ei=ei, tt=tt):
                        return e.matmul(
                            pmisc[:, 0:512], C.ones_row[:],
                            wrows[0:1, (ei - 1) * TQ + tt * 512:
                                  (ei - 1) * TQ + (tt + 1) * 512],
                            start=True, stop=True)
                    w = [(S["act"], c_routing), (S[c_pmisc_free_sem], c_pmisc_free)]
                    c_mm1 = P.op("tensor", S["pe"], w, wb_mm)

                    def wb_cp(e, tt=tt):
                        return e.activation(wb_sb[:, tt * 512:(tt + 1) * 512],
                                            pmisc[:, 0:512], AF.Copy)
                    c_wb_ready = P.op("scalar", S["act"],
                                      [(S["pe"], c_mm1), (S["dve"], last_evac)],
                                      wb_cp)
                    c_pmisc_free, c_pmisc_free_sem = c_wb_ready, "act"

            silu_cnt = {}
            mult_cnt = {}
            prod_g = {}
            prod_u = {}

            def sched_gu_dma(ci, wg_ap=wg_ap, wu_ap=wu_ap, ei=ei):
                slot = ci % 2
                hc0_, nhc = CH[ci]
                cw = nhc * 128
                prod_g[ci], x1 = wc.load(
                    P, (tag, "g", ei, ci),
                    wg_buf[slot][:, 0:KC * cw].rearrange("p (c o) -> p c o", o=cw),
                    wg_buf[slot][:, 0:KC * cw],
                    wg_ap.rearrange("(c p) o -> p c o", p=128)[
                        :, :, hc0_ * 128:(hc0_ + nhc) * 128],
                    KC * cw,
                    [(S["pe"], cons_wg[slot])] + slot_x_g[slot])
                slot_x_g[slot] = [x1] if x1 else []
                prod_u[ci], x2 = wc.load(
                    P, (tag, "u", ei, ci),
                    wu_buf[slot][:, 0:KC * cw].rearrange("p (c o) -> p c o", o=cw),
                    wu_buf[slot][:, 0:KC * cw],
                    wu_ap.rearrange("(c p) o -> p c o", p=128)[
                        :, :, hc0_ * 128:(hc0_ + nhc) * 128],
                    KC * cw,
                    [(S["pe"], cons_wu[slot])] + slot_x_u[slot])
                slot_x_u[slot] = [x2] if x2 else []

            sched_gu_dma(0)
            sched_gu_dma(1)
            sidx = 0
            for ci, (hc0_, nhc) in enumerate(CH):
                slot = ci % 2
                cw = nhc * 128
                for hcl in range(nhc):
                    hc = hc0_ + hcl
                    for tt in range(2):
                        pi = sidx % 2

                        def gmm(e, slot=slot, hcl=hcl, tt=tt, pi=pi, cw=cw):
                            for c in range(KC):
                                last = e.matmul(
                                    pg[pi][:, 0:512],
                                    wg_buf[slot][:, c * cw + hcl * 128:
                                                 c * cw + (hcl + 1) * 128],
                                    xn_q[:, c * TQ + tt * 512: c * TQ + (tt + 1) * 512],
                                    start=(c == 0), stop=(c == KC - 1))
                            return last
                        w = [prod_g[ci], (S["dve"], c_xn)]
                        if sidx == 0:
                            w.append((S["dq"], dbg_pend["g"]))
                        prev_silu = silu_cnt.get(sidx - 2)
                        if prev_silu:
                            w.append((S["act"], prev_silu))
                        elif sidx < 2:
                            w.append((S["act"], c_last_silu))
                        c_pgd = P.op("tensor", S["pe"], w, gmm)

                        def umm(e, slot=slot, hcl=hcl, tt=tt, pi=pi, cw=cw):
                            for c in range(KC):
                                last = e.matmul(
                                    pu[pi][:, 0:512],
                                    wu_buf[slot][:, c * cw + hcl * 128:
                                                 c * cw + (hcl + 1) * 128],
                                    xn_q[:, c * TQ + tt * 512: c * TQ + (tt + 1) * 512],
                                    start=(c == 0), stop=(c == KC - 1))
                            return last
                        w = [prod_u[ci]]
                        prev_mult = mult_cnt.get(sidx - 2)
                        if prev_mult:
                            w.append((S["dve"], prev_mult))
                        elif sidx < 2:
                            w.append((S["dve"], c_last_gmul))
                        c_pud = P.op("tensor", S["pe"], w, umm)
                        cons_wg[slot] = c_pud
                        cons_wu[slot] = c_pud

                        rot = sidx % 2
                        w = [(S["pe"], c_pgd)]
                        if mult_cnt.get(sidx - 2):
                            w.append((S["dve"], mult_cnt[sidx - 2]))
                        elif c_last_gmul:
                            w.append((S["dve"], c_last_gmul))
                        silu_cnt[sidx] = P.op(
                            "scalar", S["act"], w,
                            lambda e, pi=pi, rot=rot: e.activation(
                                t1_buf[rot][:], pg[pi][:, 0:512], AF.Silu))

                        mult_cnt[sidx] = P.op(
                            "vector", S["dve"],
                            [(S["pe"], c_pud), (S["act"], silu_cnt[sidx])],
                            lambda e, pi=pi, rot=rot, hc=hc, tt=tt: e.tensor_tensor(
                                g_sb[:, hc * TQ + tt * 512: hc * TQ + (tt + 1) * 512],
                                t1_buf[rot][:], pu[pi][:, 0:512], ALU.mult))
                        sidx += 1
                if ci + 2 < len(CH):
                    sched_gu_dma(ci + 2)
            c_last_silu = silu_cnt[sidx - 1]
            c_last_gmul = mult_cnt[sidx - 1]
            c_g_ready = mult_cnt[sidx - 1]
            if dbg is not None and q == 0 and ei == 0:
                dbg_pend["g"] = P.dma(
                    "scalar", S["dq"],
                    dbg["g"].rearrange("(c p) t -> p c t", p=128),
                    g_sb[:].rearrange("p (c t) -> p c t", t=TQ),
                    waits=[(S["dve"], c_g_ready)])
                tail.append((S["dq"], dbg_pend["g"]))

            prod_d = {}

            def sched_d_dma(sl4, wd_ap=wd_ap, ei=ei):
                prod_d[sl4], x1 = wc.load(
                    P, (tag, "d", ei, sl4),
                    wd_buf[sl4 % 2][:].rearrange("p (c o) -> p c o", o=256),
                    wd_buf[sl4 % 2][:],
                    wd_ap.rearrange("(c p) o -> p c o", p=128)[
                        :, :, sl4 * 256:(sl4 + 1) * 256],
                    HC0 * 256,
                    [(S["pe"], cons_wd[sl4 % 2])] + slot_x_d[sl4 % 2])
                slot_x_d[sl4 % 2] = [x1] if x1 else []

            sched_d_dma(0)
            sched_d_dma(1)
            evac_cnt = {}
            didx = 0
            for oc in range(KC):
                sl4 = oc // 2
                for tt in range(2):
                    pi = didx % 2

                    def dmm(e, oc=oc, tt=tt, pi=pi, sl4=sl4):
                        for hc in range(HC0):
                            last = e.matmul(
                                pd[pi][:, 0:512],
                                wd_buf[sl4 % 2][:, hc * 256 + (oc % 2) * 128:
                                                hc * 256 + (oc % 2 + 1) * 128],
                                g_sb[:, hc * TQ + tt * 512: hc * TQ + (tt + 1) * 512],
                                start=(hc == 0), stop=(hc == HC0 - 1))
                        return last
                    w = [prod_d[sl4]]
                    if didx == 0:
                        w.append((S["dve"], c_g_ready))
                    prev_evac = evac_cnt.get(didx - 2)
                    if prev_evac:
                        w.append((S["dve"], prev_evac))
                    elif didx < 2 and last_evac:
                        w.append((S["dve"], last_evac))
                    c_pdd = P.op("tensor", S["pe"], w, dmm)
                    cons_wd[sl4 % 2] = c_pdd

                    if shared:
                        evac_cnt[didx] = P.op(
                            "vector", S["dve"], [(S["pe"], c_pdd)],
                            lambda e, oc=oc, tt=tt, pi=pi: e.scalar_tensor_tensor(
                                x_q[:, oc * TQ + tt * 512: oc * TQ + (tt + 1) * 512],
                                pd[pi][:, 0:512], 0.5,
                                x_q[:, oc * TQ + tt * 512: oc * TQ + (tt + 1) * 512],
                                ALU.mult, ALU.add))
                    else:
                        rot = didx % 2

                        def ev(e, oc=oc, tt=tt, pi=pi, rot=rot, c_wb=c_wb_ready):
                            xsl = slice(oc * TQ + tt * 512, oc * TQ + (tt + 1) * 512)
                            e.tensor_tensor(td_buf[rot][:], pd[pi][:, 0:512],
                                            wb_sb[:, tt * 512:(tt + 1) * 512], ALU.mult)
                            return e.tensor_tensor(x_q[:, xsl], x_q[:, xsl],
                                                   td_buf[rot][:], ALU.add)
                        w = [(S["pe"], c_pdd), (S["act"], c_wb_ready)]
                        evac_cnt[didx] = P.op("vector", S["dve"], w, ev)
                    didx += 1
                if oc % 2 == 1 and sl4 + 2 < 4:
                    sched_d_dma(sl4 + 2)
            last_evac = evac_cnt[didx - 1]

        c_q_done = last_evac
        c_wb_store = P.dma("sync", S["dx"],
                           fm3(xout)[:, :, qsl],
                           x_q[:].rearrange("p (c t) -> p c t", t=TQ),
                           waits=[(S["dve"], c_q_done)])
    tail.append((S["dx"], c_wb_store))
    P.flush("ffn0" + ut)


def _prep_ffn0_w(inputs):
    return {
        "sg": np.ascontiguousarray(inputs["L0_ffn_shared_gate"], np.float32),
        "su": np.ascontiguousarray(inputs["L0_ffn_shared_up"], np.float32),
        "sd": np.ascontiguousarray(inputs["L0_ffn_shared_down"], np.float32),
        "rg": np.ascontiguousarray(inputs["L0_ffn_routed_gate"], np.float32),
        "ru": np.ascontiguousarray(inputs["L0_ffn_routed_up"], np.float32),
        "rd": np.ascontiguousarray(inputs["L0_ffn_routed_down"], np.float32),
        "wr": np.ascontiguousarray(inputs["L0_ffn_Wr"], np.float32),
        "rb": np.ascontiguousarray(inputs["L0_ffn_routing_bias"], np.float32),
    }


def build_ffn0_program(debug=False):
    nc = bass.Bass("TRN2", target_bir_lowering=False, debug=False)
    xin = nc.dram_tensor("x", [D, T], F32, kind="ExternalInput").ap()
    xout = nc.dram_tensor("y", [D, T], F32, kind="ExternalOutput").ap()
    dbg = None
    if debug:
        dbg = {
            "xn": nc.dram_tensor("xn_dump", [D, T], BF16, kind="ExternalOutput").ap(),
            "w": nc.dram_tensor("w_dump", [NEXP, T], BF16, kind="ExternalOutput").ap(),
            "g": nc.dram_tensor("g_dump", [HID0, TQ], BF16, kind="ExternalOutput").ap(),
            "rt": True,
            "rt_lg": nc.dram_tensor("rt_lg", [128, NEXP], F32, kind="ExternalOutput").ap(),
            "rt_probs": nc.dram_tensor("rt_probs", [128, NEXP], F32, kind="ExternalOutput").ap(),
            "rt_rbb": nc.dram_tensor("rt_rbb", [128, NEXP], F32, kind="ExternalOutput").ap(),
            "rt_keep": nc.dram_tensor("rt_keep", [128, NEXP], F32, kind="ExternalOutput").ap(),
            "rt_den": nc.dram_tensor("rt_den", [128, 1], F32, kind="ExternalOutput").ap(),
            "rt_wfin": nc.dram_tensor("rt_wfin", [128, NEXP], BF16, kind="ExternalOutput").ap(),
        }
    W = {
        "sg": nc.dram_tensor("sg", [D, HID0], F32, kind="ExternalInput").ap(),
        "su": nc.dram_tensor("su", [D, HID0], F32, kind="ExternalInput").ap(),
        "sd": nc.dram_tensor("sd", [HID0, D], F32, kind="ExternalInput").ap(),
        "rg": nc.dram_tensor("rg", [NEXP, D, HID0], F32, kind="ExternalInput").ap(),
        "ru": nc.dram_tensor("ru", [NEXP, D, HID0], F32, kind="ExternalInput").ap(),
        "rd": nc.dram_tensor("rd", [NEXP, HID0, D], F32, kind="ExternalInput").ap(),
        "wr": nc.dram_tensor("wr", [D, NEXP], F32, kind="ExternalInput").ap(),
        "rb": nc.dram_tensor("rb", [NEXP], F32, kind="ExternalInput").ap(),
    }
    W["wc"] = WCache(nc, "wcache", 540672 + 64)
    W["tag"] = "w0"
    with ExitStack() as stack:
        P = Prog(nc, stack)
        C = Consts(nc, P, stack, need_attn=False)
        tail = []
        emit_ffn0(nc, P, C, stack, xin, xout, W, tail, dbg=dbg)
        with nc.Block("fin") as blk:
            @blk.sync
            def _(eng):
                for sem, cnt in tail:
                    eng.wait_ge(sem.h, cnt)
    return nc


TG = 512


def emit_ffn1(nc, P, C, stack, xin, xout, W, tail, dbg=None):
    S = P.s
    ut = _tag()
    sb = lambda n, shp, dt: stack.enter_context(nc.sbuf_tensor(n + ut, shp, dt))
    x_g = sb("f1_x", [128, KC * TG], F32)
    xn_g = sb("f1_xn", [128, KC * TG], BF16)
    h_a = sb("f1_ha", [128, HC1 * TG], BF16)
    h_b = sb("f1_hb", [128, HC1 * TG], BF16)
    w1_buf = [sb(f"f1_w1{i}", [128, KC * 512], BF16) for i in range(2)]
    w2_buf = [sb(f"f1_w2{i}", [128, KC * 512], BF16) for i in range(2)]
    hh_buf = [sb(f"f1_hh{i}", [128, HC1 * 256], BF16) for i in range(2)]
    w3_buf = [sb(f"f1_w3{i}", [128, HC1 * 128], BF16) for i in range(2)]
    t1_buf = [sb(f"f1_t1{i}", [128, 512], BF16) for i in range(2)]
    sc = {n: sb("f1_s_" + n, [128, HC1], F32) for n in ("s1", "s2", "sh1", "sh2")}
    s3h = sb("f1_s3h", [128, KC], F32)
    s3t = sb("f1_s3t", [128, KC], F32)

    ps = lambda n: stack.enter_context(nc.psum_tensor(n + ut, [128, 512], F32))
    pp = [ps(f"f1_p{i}") for i in range(6)]
    lnp = make_lnp(nc, stack, "f1" + ut, TG, xn_g[:, 0:KC * TG],
                   {"s1": pp[4][0:1, 0:512], "s2": pp[5][0:1, 0:512],
                    "mb": pp[0][:, 0:512], "rb": pp[1][:, 0:512]})
    lnp["sq"] = h_a[:, 0:KC * TG]

    c_s = 0
    for n in ("s1", "s2", "sh1", "sh2"):
        for hc in range(HC1):
            c_s = P.dma("sync", S["dx"], sc[n][:, hc:hc + 1],
                        W[n].rearrange("(a b) -> a b", b=1)[
                            hc * 128:(hc + 1) * 128, :])
    c_s3 = 0
    for oc in range(KC):
        c_s3 = P.dma("sync", S["dx"], s3t[:, oc:oc + 1],
                     W["s3"].rearrange("(a b) -> a b", b=1)[
                         oc * 128:(oc + 1) * 128, :])
    c_s3h = P.op("vector", S["dve"], [(S["dx"], c_s3)],
                 lambda e: e.tensor_scalar(s3h[:], s3t[:], 0.5, None, ALU.mult))

    wc = W["wc"]
    tag = W["tag"]
    slot_x_1 = [[], []]; slot_x_2 = [[], []]; slot_x_h = [[], []]; slot_x_3 = [[], []]
    cons_w1 = [0, 0]; cons_w2 = [0, 0]; cons_hh = [0, 0]; cons_w3 = [0, 0]
    c_g_done = 0
    c_last_silu = 0
    c_last_stt = 0
    c_wb_store = 0
    c_ln_var = 0
    NCH12 = 8
    NCHH = 16
    NCH3 = 8

    for g in range(T // TG):
        gsl = slice(g * TG, (g + 1) * TG)
        c_x = P.dma("sync", S["dx"], x_g[:].rearrange("p (c t) -> p c t", t=TG),
                    fm3(xin)[:, :, gsl],
                    waits=[(S["dve"], c_g_done), (S["dx"], c_wb_store)])
        ln = emit_ln(P, C, lnp, x_g, xn_g, TG, (S["dx"], c_x),
                     pe_stat_waits=[(S["dve"], c_g_done), (S["dve"], c_ln_var)],
                     pe_bc_waits=[(S["act"], c_last_silu)])
        c_xn = ln["xn"]
        c_ln_var = ln["var"]

        prod_1 = {}
        prod_2 = {}

        def sched_12(ci):
            slot = ci % 2
            prod_1[ci], x1 = wc.load(
                P, (tag, "w1", ci),
                w1_buf[slot][:].rearrange("p (c o) -> p c o", o=512),
                w1_buf[slot][:],
                W["w1"].rearrange("(c p) o -> p c o", p=128)[
                    :, :, ci * 512:(ci + 1) * 512],
                KC * 512,
                [(S["pe"], cons_w1[slot])] + slot_x_1[slot])
            slot_x_1[slot] = [x1] if x1 else []
            prod_2[ci], x2 = wc.load(
                P, (tag, "w2", ci),
                w2_buf[slot][:].rearrange("p (c o) -> p c o", o=512),
                w2_buf[slot][:],
                W["w2"].rearrange("(c p) o -> p c o", p=128)[
                    :, :, ci * 512:(ci + 1) * 512],
                KC * 512,
                [(S["pe"], cons_w2[slot])] + slot_x_2[slot])
            slot_x_2[slot] = [x2] if x2 else []

        sched_12(0)
        sched_12(1)
        silu_cnt = {}
        stt_cnt = {}
        sidx = 0
        for ci in range(NCH12):
            slot = ci % 2
            for hcl in range(4):
                hc = ci * 4 + hcl
                pi = sidx % 2

                def mm1(e, slot=slot, hcl=hcl, pi=pi):
                    for c in range(KC):
                        last = e.matmul(
                            pp[pi][:, 0:512],
                            w1_buf[slot][:, c * 512 + hcl * 128: c * 512 + (hcl + 1) * 128],
                            xn_g[:, c * TG: (c + 1) * TG],
                            start=(c == 0), stop=(c == KC - 1))
                    return last
                w = [prod_1[ci], (S["dve"], c_xn)]
                prev_silu = silu_cnt.get(sidx - 2)
                if prev_silu:
                    w.append((S["act"], prev_silu))
                elif c_last_silu:
                    w.append((S["act"], c_last_silu))
                c_p1 = P.op("tensor", S["pe"], w, mm1)

                def mm2(e, slot=slot, hcl=hcl, pi=pi):
                    for c in range(KC):
                        last = e.matmul(
                            pp[2 + pi][:, 0:512],
                            w2_buf[slot][:, c * 512 + hcl * 128: c * 512 + (hcl + 1) * 128],
                            xn_g[:, c * TG: (c + 1) * TG],
                            start=(c == 0), stop=(c == KC - 1))
                    return last
                w = [prod_2[ci]]
                prev_stt = stt_cnt.get(sidx - 2)
                if prev_stt:
                    w.append((S["dve"], prev_stt))
                elif c_last_stt:
                    w.append((S["dve"], c_last_stt))
                c_p2 = P.op("tensor", S["pe"], w, mm2)
                cons_w1[slot] = c_p2
                cons_w2[slot] = c_p2

                rot = sidx % 2
                w = [(S["pe"], c_p1), (S["dx"], c_s)]
                if stt_cnt.get(sidx - 2):
                    w.append((S["dve"], stt_cnt[sidx - 2]))
                elif c_last_stt:
                    w.append((S["dve"], c_last_stt))
                silu_cnt[sidx] = P.op(
                    "scalar", S["act"], w,
                    lambda e, pi=pi, rot=rot, hc=hc: e.activation(
                        t1_buf[rot][:], pp[pi][:, 0:512], AF.Silu,
                        scale=sc["s1"][:, hc:hc + 1]))

                stt_cnt[sidx] = P.op(
                    "vector", S["dve"],
                    [(S["pe"], c_p2), (S["act"], silu_cnt[sidx])],
                    lambda e, pi=pi, rot=rot, hc=hc: e.scalar_tensor_tensor(
                        h_a[:, hc * TG:(hc + 1) * TG], pp[2 + pi][:, 0:512],
                        sc["s2"][:, hc:hc + 1], t1_buf[rot][:],
                        ALU.mult, ALU.mult))
                sidx += 1
            if ci + 2 < NCH12:
                sched_12(ci + 2)
        c_last_silu = silu_cnt[sidx - 1]
        c_last_stt = stt_cnt[sidx - 1]
        c_h_ready = stt_cnt[sidx - 1]

        for ph, (wkey, skey, src_h, dst_h) in enumerate(
                [("h1", "sh1", h_a, h_b), ("h2", "sh2", h_b, h_a)]):
            prod_h = {}

            def sched_h(ci, wkey=wkey):
                slot = ci % 2
                prod_h[ci], x1 = wc.load(
                    P, (tag, wkey, ci),
                    hh_buf[slot][:].rearrange("p (c o) -> p c o", o=256),
                    hh_buf[slot][:],
                    W[wkey].rearrange("(c p) o -> p c o", p=128)[
                        :, :, ci * 256:(ci + 1) * 256],
                    HC1 * 256,
                    [(S["pe"], cons_hh[slot])] + slot_x_h[slot])
                slot_x_h[slot] = [x1] if x1 else []

            sched_h(0)
            sched_h(1)
            gelu_cnt = {}
            hidx = 0
            for ci in range(NCHH):
                slot = ci % 2
                for ocl in range(2):
                    oc = ci * 2 + ocl
                    pi = hidx % 2

                    def mmh(e, slot=slot, ocl=ocl, pi=pi, src_h=src_h):
                        for hc in range(HC1):
                            last = e.matmul(
                                pp[pi][:, 0:512],
                                hh_buf[slot][:, hc * 256 + ocl * 128:
                                             hc * 256 + (ocl + 1) * 128],
                                src_h[:, hc * TG:(hc + 1) * TG],
                                start=(hc == 0), stop=(hc == HC1 - 1))
                        return last
                    w = [prod_h[ci]]
                    if hidx == 0:
                        w.append((S["dve"] if ph == 0 else S["act"],
                                  c_h_ready))
                    prev_gelu = gelu_cnt.get(hidx - 2)
                    if prev_gelu:
                        w.append((S["act"], prev_gelu))
                    elif hidx < 2 and c_last_silu:
                        w.append((S["act"], c_last_silu))
                    c_ph = P.op("tensor", S["pe"], w, mmh)
                    cons_hh[slot] = c_ph

                    gelu_cnt[hidx] = P.op(
                        "scalar", S["act"], [(S["pe"], c_ph)],
                        lambda e, pi=pi, oc=oc, skey=skey, dst_h=dst_h: e.activation(
                            dst_h[:, oc * TG:(oc + 1) * TG], pp[pi][:, 0:512],
                            AF.Gelu, scale=sc[skey][:, oc:oc + 1]))
                    hidx += 1
                if ci + 2 < NCHH:
                    sched_h(ci + 2)
            c_h_ready = gelu_cnt[hidx - 1]
            c_last_silu = gelu_cnt[hidx - 1]

        prod_3 = {}

        def sched_3(ci):
            slot = ci % 2
            prod_3[ci], x1 = wc.load(
                P, (tag, "w3", ci),
                w3_buf[slot][:].rearrange("p (c o) -> p c o", o=128),
                w3_buf[slot][:],
                W["w3"].rearrange("(c p) o -> p c o", p=128)[
                    :, :, ci * 128:(ci + 1) * 128],
                HC1 * 128,
                [(S["pe"], cons_w3[slot])] + slot_x_3[slot])
            slot_x_3[slot] = [x1] if x1 else []

        sched_3(0)
        sched_3(1)
        ev3_cnt = {}
        for oc in range(NCH3):
            slot = oc % 2
            pi = oc % 2

            def mm3(e, slot=slot, pi=pi):
                for hc in range(HC1):
                    last = e.matmul(
                        pp[2 + pi][:, 0:512],
                        w3_buf[slot][:, hc * 128:(hc + 1) * 128],
                        h_a[:, hc * TG:(hc + 1) * TG],
                        start=(hc == 0), stop=(hc == HC1 - 1))
                return last
            w = [prod_3[oc]]
            if oc == 0:
                w.append((S["act"], c_h_ready))
            prev_ev = ev3_cnt.get(oc - 2)
            if prev_ev:
                w.append((S["dve"], prev_ev))
            elif oc < 2 and c_last_stt:
                w.append((S["dve"], c_last_stt))
            c_p3 = P.op("tensor", S["pe"], w, mm3)
            cons_w3[slot] = c_p3

            ev3_cnt[oc] = P.op(
                "vector", S["dve"],
                [(S["pe"], c_p3), (S["dve"], c_s3h)],
                lambda e, oc=oc, pi=pi: e.scalar_tensor_tensor(
                    x_g[:, oc * TG:(oc + 1) * TG], pp[2 + pi][:, 0:512],
                    s3h[:, oc:oc + 1],
                    x_g[:, oc * TG:(oc + 1) * TG], ALU.mult, ALU.add))
            if oc + 2 < NCH3:
                sched_3(oc + 2)
        c_last_stt = ev3_cnt[NCH3 - 1]
        c_g_done = ev3_cnt[NCH3 - 1]
        c_wb_store = P.dma("sync", S["dx"],
                           fm3(xout)[:, :, gsl],
                           x_g[:].rearrange("p (c t) -> p c t", t=TG),
                           waits=[(S["dve"], c_g_done)])
        if dbg is not None and g == 0:
            tail.append((S["dq"], P.dma(
                "scalar", S["dq"], dbg["h"].rearrange("(c p) t -> p c t", p=128),
                h_b[:].rearrange("p (c t) -> p c t", t=TG),
                waits=[(S["act"], c_h_ready)])))
    tail.append((S["dx"], c_wb_store))
    P.flush("ffn1" + ut)


def build_ffn1_program(debug=False):
    nc = bass.Bass("TRN2", target_bir_lowering=False, debug=False)
    xin = nc.dram_tensor("x", [D, T], F32, kind="ExternalInput").ap()
    xout = nc.dram_tensor("y", [D, T], F32, kind="ExternalOutput").ap()
    W = {
        "w1": nc.dram_tensor("w1", [D, HID1], F32, kind="ExternalInput").ap(),
        "w2": nc.dram_tensor("w2", [D, HID1], F32, kind="ExternalInput").ap(),
        "h1": nc.dram_tensor("h1", [HID1, HID1], F32, kind="ExternalInput").ap(),
        "h2": nc.dram_tensor("h2", [HID1, HID1], F32, kind="ExternalInput").ap(),
        "w3": nc.dram_tensor("w3", [HID1, D], F32, kind="ExternalInput").ap(),
        "s1": nc.dram_tensor("s1", [HID1], F32, kind="ExternalInput").ap(),
        "s2": nc.dram_tensor("s2", [HID1], F32, kind="ExternalInput").ap(),
        "sh1": nc.dram_tensor("sh1", [HID1], F32, kind="ExternalInput").ap(),
        "sh2": nc.dram_tensor("sh2", [HID1], F32, kind="ExternalInput").ap(),
        "s3": nc.dram_tensor("s3", [D], F32, kind="ExternalInput").ap(),
    }
    dbg = None
    if debug:
        dbg = {"h": nc.dram_tensor("h_dump", [HID1, TG], BF16,
                                   kind="ExternalOutput").ap()}
    W["wc"] = WCache(nc, "wcache", 360448 + 64)
    W["tag"] = "w1"
    with ExitStack() as stack:
        P = Prog(nc, stack)
        C = Consts(nc, P, stack, need_attn=False)
        tail = []
        emit_ffn1(nc, P, C, stack, xin, xout, W, tail, dbg=dbg)
        with nc.Block("fin") as blk:
            @blk.sync
            def _(eng):
                for sem, cnt in tail:
                    eng.wait_ge(sem.h, cnt)
    return nc



def emit_attn(nc, P, C, stack, xin, xout, W, tail, dbg=None):
    S = P.s
    ut = _tag()
    sb = lambda n, shp, dt: stack.enter_context(nc.sbuf_tensor(n + ut, shp, dt))
    x_t = sb("at_x", [128, KC * 256], F32)
    xn_t = [sb(f"at_xn{i}", [128, KC * 512], BF16) for i in range(2)]
    kf = sb("at_kf", [128, KC * T], BF16)
    vtok = sb("at_vtok", [128, 32 * (NH * 65)], BF16)
    wq_buf = [sb(f"at_wq{i}", [128, KC * 512], BF16) for i in range(2)]
    k_ext = sb("at_kext", [128, T], BF16)
    q_ext = sb("at_qext", [128, T], BF16)
    p_sb = [sb(f"at_pp{i}", [128, 512], BF16) for i in range(6)]
    psb3 = p_sb[3]
    qstage = [p_sb[0][:, 0:512], p_sb[1][:, 0:512]]
    vstage = p_sb[2][:, 0:512]
    att_sb = [sb(f"at_att{i}", [64, 512], BF16) for i in range(2)]
    lf32_0 = sb("at_lf32", [1, 512], F32)
    linv_bf = sb("at_linv", [1, 512], BF16)
    linvb_sb = sb("at_linvb", [64, 512], BF16)
    m_f32 = sb("at_m", [128, 16], F32)
    mcols = sb("at_mcols", [128, 8], F32)
    mtmp = sb("at_mtmp", [128, 1], F32)
    mneg = sb("at_mneg", [128, 16], BF16)
    bias = {}
    if W["has_bias"]:
        for n in ("bq8", "bk", "bv", "bo"):
            bias[n] = sb("at_" + n, [128, KC], F32)
    ps = lambda n: stack.enter_context(nc.psum_tensor(n + ut, [128, 512], F32))
    pp = [ps(f"at_ps{i}") for i in range(5)]
    lf32 = [lf32_0, None]
    lf32[1] = x_t[0:1, 0:512]
    linv32 = x_t[64:65, 0:512]
    pst = [ps(f"at_pst{i}") for i in range(3)]
    pvtr = pp[4]
    lnp = make_lnp(nc, stack, "at" + ut, 512, k_ext[:, 0:KC * 512],
                   {"s1": pst[0][0:1, 0:512], "s2": pst[1][0:1, 0:512],
                    "mb": pst[0][:, 0:512], "rb": pst[1][:, 0:512]},
                   mb_alias=q_ext[:, 0:512], rb_alias=q_ext[:, 512:1024],
                   nm_alias=psb3[0:1, 0:512], r_alias=att_sb[0][0:1, 0:512],
                   m32_alias=x_t[0:1, 512:1024], var_alias=x_t[0:1, 1024:1536])

    c_bias = 0
    c_bias_dve = 0
    if W["has_bias"]:
        for n in ("bq8", "bk", "bv", "bo"):
            for ocb in range(KC):
                c_bias = P.dma("sync", S["dx"], bias[n][:, ocb:ocb + 1],
                               W[n].rearrange("(a b) -> a b", b=1)[
                                   ocb * 128:(ocb + 1) * 128, :])
        c_bias_dve = P.op("vector", S["dve"], [(S["dx"], c_bias)],
                          lambda e: e.tensor_scalar(bias["bq8"][:], bias["bq8"][:],
                                                    0.125, None, ALU.mult))
    c_vones = P.op("vector", S["dve"], [], lambda e: e.memset(vtok[:], 1.0))

    wc = W["wc"]
    tag = W["tag"]
    slot_x_q = [[], []]
    NOC = 3 * KC
    NCHQ = NOC // 4
    cons_wq = [0, 0]
    prod_q = {}

    def sched_q_dma(ci):
        slot = ci % 2
        ap2d, col0 = W["qkv_src"](ci * 4)
        prod_q[ci], x1 = wc.load(
            P, (tag, "qkv", ci),
            wq_buf[slot][:].rearrange("p (c o) -> p c o", o=512),
            wq_buf[slot][:],
            ap2d.rearrange("(c p) o -> p c o", p=128)[:, :, col0:col0 + 512],
            KC * 512,
            [(S["pe"], cons_wq[slot])] + slot_x_q[slot])
        slot_x_q[slot] = [x1] if x1 else []

    c_ln_var = 0
    c_ln_bc = 0
    c_mm_last = [0, 0]
    c_qst_dma = [0, 0]
    c_vtr_prev = 0
    c_vcp_prev = c_vones
    c_ev_hist = {}
    for tt in range(T // 512):
        tsl = slice(tt * 512, (tt + 1) * 512)
        xb = tt % 2
        c_x = P.dma("gpsimd", S["dw"],
                    xn_t[xb][:].rearrange("p (c t) -> p c t", t=512),
                    fm3(xin)[:, :, tsl], waits=[(S["pe"], c_mm_last[xb])])
        ln = emit_ln(P, C, lnp, None, xn_t[xb], 512, (S["dw"], c_x),
                     pe_stat_waits=[(S["dve"], c_ln_var)],
                     pe_bc_waits=[(S["act"], c_ln_bc)],
                     pre_cast=True)
        c_xn = ln["xn"]
        c_ln_var = ln["var"]
        c_ln_bc = ln["act_bc"]
        for ci in range(2):
            sched_q_dma(ci)
        for oc in range(NOC):
            ci = oc // 4
            slot = ci % 2
            pi = oc % 2

            def qkv_mm(e, oc=oc, slot=slot, pi=pi, xb=xb):
                for c in range(KC):
                    last = e.matmul(
                        pp[pi][:, 0:512],
                        wq_buf[slot][:, c * 512 + (oc % 4) * 128:
                                     c * 512 + (oc % 4 + 1) * 128],
                        xn_t[xb][:, c * 512:(c + 1) * 512],
                        start=(c == 0), stop=(c == KC - 1))
                return last
            w = [prod_q[ci], (S["dve"], c_xn)]
            prev_ev = c_ev_hist.get((tt, oc - 2)) or c_ev_hist.get(
                (tt - 1, oc + NOC - 2))
            if prev_ev:
                w.append(prev_ev)
            c_mm = P.op("tensor", S["pe"], w, qkv_mm)
            c_mm_last[xb] = c_mm
            cons_wq[slot] = c_mm
            if oc % 4 == 3 and ci + 2 < NCHQ:
                sched_q_dma(ci + 2)

            if oc < KC:
                qi = oc % 2

                def q_ev(e, pi=pi, qi=qi, oc=oc):
                    if W["has_bias"]:
                        return e.tensor_scalar(qstage[qi][:], pp[pi][:, 0:512],
                                               0.125, bias["bq8"][:, oc:oc + 1],
                                               ALU.mult, ALU.add)
                    return e.tensor_scalar(qstage[qi][:], pp[pi][:, 0:512],
                                           0.125, None, ALU.mult)
                c_ev = P.op("vector", S["dve"],
                            [(S["pe"], c_mm), (S["dx"], c_qst_dma[qi]),
                             (S["dve"], c_bias_dve)], q_ev)
                c_ev_hist[(tt, oc)] = (S["dve"], c_ev)
                c_qst_dma[qi] = P.dma(
                    "sync", S["dx"],
                    W["qf"].rearrange("(c p) t -> p c t", p=128)[:, oc:oc + 1, tsl],
                    qstage[qi][:].rearrange("p (c t) -> p c t", c=1),
                    waits=[(S["dve"], c_ev)])
            elif oc < 2 * KC:
                kc_out = oc - KC

                def k_ev(e, pi=pi, kc_out=kc_out, tt=tt):
                    dst = kf[:, kc_out * T + tt * 512: kc_out * T + (tt + 1) * 512]
                    if W["has_bias"]:
                        return e.tensor_scalar(dst, pp[pi][:, 0:512],
                                               bias["bk"][:, kc_out:kc_out + 1],
                                               None, ALU.add)
                    return e.tensor_copy(dst, pp[pi][:, 0:512])
                c_ev = P.op("vector", S["dve"], [(S["pe"], c_mm)], k_ev)
                c_ev_hist[(tt, oc)] = (S["dve"], c_ev)
            else:
                vc = oc - 2 * KC

                def v_ev(e, pi=pi, vc=vc):
                    if W["has_bias"]:
                        return e.tensor_scalar(vstage[:], pp[pi][:, 0:512],
                                               bias["bv"][:, vc:vc + 1],
                                               None, ALU.add)
                    return e.tensor_copy(vstage[:], pp[pi][:, 0:512])
                c_ev = P.op("vector", S["dve"],
                            [(S["pe"], c_mm), (S["pe"], c_vtr_prev)], v_ev)

                def v_tr(e):
                    for tb in range(4):
                        last = e.matmul(pvtr[:, tb * 128:(tb + 1) * 128],
                                        vstage[:, tb * 128:(tb + 1) * 128],
                                        C.ident[:], start=True, stop=True)
                    return last
                c_tr = P.op("tensor", S["pe"],
                            [(S["dve"], c_ev), (S["dve"], c_vcp_prev)], v_tr)
                c_vtr_prev = c_tr

                def v_cp(e, vc=vc, tt=tt):
                    for tb in range(4):
                        base = (tt * 4 + tb) * (NH * 65)
                        e.tensor_copy(
                            vtok[:, base + 2 * vc * 65: base + 2 * vc * 65 + 64],
                            pvtr[:, tb * 128: tb * 128 + 64])
                        last = e.tensor_copy(
                            vtok[:, base + (2 * vc + 1) * 65:
                                 base + (2 * vc + 1) * 65 + 64],
                            pvtr[:, tb * 128 + 64: (tb + 1) * 128])
                    return last
                c_vcp = P.op("vector", S["dve"],
                             [(S["pe"], c_tr), (S["dve"], c_vones)], v_cp)
                c_vcp_prev = c_vcp
                c_ev_hist[(tt, oc)] = (S["pe"], c_tr)
    c_phaseA_done_dve = c_vcp_prev
    P.flush("attn_A" + ut)

    c_kones = P.op("vector", S["dve"], [], lambda e: (
        e.memset(k_ext[64:65, :], 1.0),
        e.memset(xn_t[0][64:65, :], 1.0))[-1])
    ke = [k_ext, xn_t[0]]
    qe = [q_ext, xn_t[1]]
    p_sb6 = [p_sb[i][:, 0:512] for i in range(6)]
    c_st_last = 0
    c_mrow_cp = 0
    c_lb_cp = 0
    c_orel = [0, 0]
    c_nmult = [0, 0]
    c_att_dma = [0, 0]
    red_hist = {}
    exp_hist = {}
    pv_hist = {}
    ridx_ctr = [0]
    NG = T // 512
    pend_tail = [None]
    for j in range(NH // 2):
        cc_kq = []
        for hp in range(2):
            h = 2 * j + hp
            c_kc = P.op("vector", S["dve"],
                        [(S["pe"], c_st_last), (S["dve"], c_kones)],
                        lambda e, h=h, hp=hp: e.tensor_copy(
                            ke[hp][0:64, :],
                            kf[(h % 2) * 64:(h % 2) * 64 + 64,
                               (h // 2) * T:(h // 2 + 1) * T]))
            c_ql = P.dma("sync", S["dx"],
                         qe[hp][0:64, :].rearrange("p (c t) -> p c t", c=1),
                         W["qf"].rearrange("(x p) t -> p x t", p=64)[:, h:h + 1, :],
                         waits=[(S["pe"], c_st_last)])
            cc_kq.append((c_kc, c_ql))

        p1_done = {}

        p1_thunks = []

        def p1_unit(g, hp, qt4, kb, j=j):
            nonlocal c_st_last
            qb = g * 512
            mb8 = 8 * hp + 4 * (g % 2)
            c_kc, c_ql = cc_kq[hp]
            ri = ridx_ctr[0]
            ridx_ctr[0] += 1
            pi = ri % 2

            def s_mm(e, qt4=qt4, kb=kb, pi=pi, qb=qb, hp=hp):
                return e.matmul(
                    pp[pi][:, 0:512],
                    qe[hp][0:64, qb + qt4 * 128: qb + (qt4 + 1) * 128],
                    ke[hp][0:64, kb * 512:(kb + 1) * 512],
                    start=True, stop=True)
            w = []
            if g == 0 and qt4 == 0 and kb == 0:
                w += [(S["dx"], c_ql), (S["dve"], c_kc)]
            prev = red_hist.get(ri - 2)
            if prev:
                w.append((S["dve"], prev))
            c_s = P.op("tensor", S["pe"], w, s_mm)
            c_st_last = c_s

            def red(e, qt4=qt4, kb=kb, pi=pi, mb8=mb8, g=g):
                if g == 0:
                    return e.tensor_reduce(m_f32[:, mb8 + qt4:mb8 + qt4 + 1],
                                           pp[pi][:, 0:512], AX.X, ALU.max)
                last = e.tensor_reduce(mcols[:, kb:kb + 1], pp[pi][:, 0:512],
                                       AX.X, ALU.max)
                if kb == g:
                    last = e.tensor_reduce(
                        m_f32[:, mb8 + qt4:mb8 + qt4 + 1],
                        mcols[:, 0:g + 1], AX.X, ALU.max)
                return last
            red_hist[ri] = P.op("vector", S["dve"], [(S["pe"], c_s)], red)
            p1_done[(g, hp)] = red_hist[ri]

        def emit_pass1(g, hp, defer=False):
            units = [(g, hp, qt4, kb) for qt4 in range(4) for kb in range(g + 1)]
            if defer:
                p1_thunks.extend(units)
            else:
                for u in units:
                    p1_unit(*u)

        def drain_p1(n):
            for _ in range(min(n, len(p1_thunks))):
                p1_unit(*p1_thunks.pop(0))

        def emit_mrow(g, hp, j=j):
            nonlocal c_mrow_cp
            qb = g * 512
            mb8 = 8 * hp + 4 * (g % 2)
            c_mneg = P.op("vector", S["dve"], [(S["dve"], p1_done[(g, hp)])],
                          lambda e, mb8=mb8: e.tensor_scalar(
                              mneg[:, mb8:mb8 + 4], m_f32[:, mb8:mb8 + 4],
                              -1.0, None, ALU.mult))

            def mrow_mms(e, mb8=mb8):
                for qt4 in range(4):
                    last = e.matmul(pp[4][0:1, qt4 * 128:(qt4 + 1) * 128],
                                    mneg[:, mb8 + qt4:mb8 + qt4 + 1], C.ident[:],
                                    start=True, stop=True)
                return last
            c_mrow = P.op("tensor", S["pe"],
                          [(S["dve"], c_mneg), (S["act"], c_mrow_cp),
                           (S["act"], c_lb_cp)], mrow_mms)
            c_mrow_cp = P.op("scalar", S["act"], [(S["pe"], c_mrow)],
                             lambda e, qb=qb, hp=hp: e.activation(
                                 qe[hp][64:65, qb:qb + 512], pp[4][0:1, 0:512],
                                 AF.Copy))
            mrow_done[(g, hp)] = c_mrow_cp
            mrow_done["latest"] = c_mrow_cp

        def release_O(g, hp):
            if g < 0:
                return
            nkb = 4 * (g + 1)
            c = P.op("scalar", S["act"],
                     [(S["pe"], pv_hist[(hp, g, nkb - 1)]),
                      (S["dx"], c_att_dma[hp])],
                     lambda e, hp=hp: (
                         e.activation(att_sb[hp][:], pp[2 + hp][0:64, 0:512],
                                      AF.Copy),
                         e.activation(lf32[hp][0:1, :], pp[2 + hp][64:65, 0:512],
                                      AF.Copy))[-1])
            c_orel[hp] = c

        def norm_tail(g, hp, j=j):
            nonlocal c_lb_cp
            if g < 0:
                return
            h = 2 * j + hp
            qb = g * 512
            c_li = P.op("vector", S["dve"], [(S["act"], c_orel[hp])],
                        lambda e, hp=hp: (
                            e.reciprocal(linv32[0:1, :], lf32[hp][0:1, :]),
                            e.tensor_copy(linv_bf[0:1, :], linv32[0:1, :]))[-1])
            c_lb = P.op("tensor", S["pe"],
                        [(S["dve"], c_li), (S["act"], c_lb_cp),
                         (S["act"], mrow_done.get("latest", 0))],
                        lambda e: e.matmul(pp[4][0:64, 0:512],
                                           C.ones_row[0:1, 0:64], linv_bf[0:1, :],
                                           start=True, stop=True))
            c_lb_cp = P.op("scalar", S["act"],
                           [(S["pe"], c_lb), (S["dve"], c_nmult[1 - hp])],
                           lambda e: e.activation(linvb_sb[:], pp[4][0:64, 0:512],
                                                  AF.Copy))
            c_m = P.op("vector", S["dve"], [(S["act"], c_lb_cp)],
                       lambda e, hp=hp: e.tensor_tensor(
                           att_sb[hp][:], att_sb[hp][:], linvb_sb[:], ALU.mult))
            c_nmult[hp] = c_m
            c_att_dma[hp] = P.dma(
                "sync", S["dx"],
                W["af"].rearrange("(x p) t -> p x t", p=64)[:, h:h + 1, qb:qb + 512],
                att_sb[hp][:].rearrange("p (c t) -> p c t", c=1),
                waits=[(S["dve"], c_m)])

        mrow_done = {}
        st_cnt = {}
        st_slot = {}
        st_issue = [0]
        exp_by_issue = {}

        def emit_st(g, hp, kb, waits=None):
            nonlocal c_st_last
            qb = g * 512
            si = st_issue[0]
            st_issue[0] += 1
            slot = si % 3
            st_slot[(hp, kb)] = (slot, si)

            def st_mm(e, kb=kb, hp=hp, qb=qb, slot=slot):
                return e.matmul(
                    pst[slot][:, 0:512],
                    ke[hp][0:65, kb * 128:(kb + 1) * 128],
                    qe[hp][0:65, qb:qb + 512],
                    start=True, stop=True)
            if waits is None:
                w = []
                if kb == 0:
                    w.append((S["act"], mrow_done[(g, hp)]))
                prev_exp = exp_by_issue.get(si - 3)
                if prev_exp:
                    w.append((S["act"], prev_exp))
            else:
                w = list(waits)
                if kb == 0:
                    w.append((S["act"], mrow_done[(g, hp)]))
            st_cnt[(hp, kb)] = P.op("tensor", S["pe"], w, st_mm)
            c_st_last = st_cnt[(hp, kb)]

        emit_pass1(0, 0)
        emit_pass1(0, 1)
        if pend_tail[0] is not None:
            pend_tail[0]()
            pend_tail[0] = None
        for g in range(NG):
            if g + 1 < NG:
                emit_pass1(g + 1, 0, defer=True)
                emit_pass1(g + 1, 1, defer=True)
            emit_mrow(g, 0)
            emit_mrow(g, 1)
            release_O(g - 1, 0)
            release_O(g - 1, 1)
            nkb = 4 * (g + 1)
            emit_st(g, 0, 0)
            emit_st(g, 1, 0)

            def emit_pv(kb, hp, g=g, j=j, nkb=nkb, waits=None):
                pr = hp * 3 + kb % 3
                rdy = msk_or_exp[(hp, kb)]

                def pv_mm(e, kb=kb, hp=hp, pr=pr, j=j,
                          first=(kb == 0), last=(kb == nkb - 1)):
                    h = 2 * j + hp
                    return e.matmul(
                        pp[2 + hp][0:65, 0:512],
                        vtok[:, kb * (NH * 65) + h * 65:
                             kb * (NH * 65) + (h + 1) * 65],
                        p_sb6[pr][:],
                        start=first, stop=last)
                if waits is None:
                    w = [(S[rdy[0]], rdy[1])]
                    if kb == 0:
                        w.append((S["act"], c_orel[hp]))
                else:
                    w = list(waits)
                pv_hist[(hp, g, kb)] = P.op("tensor", S["pe"], w, pv_mm)

            msk_or_exp = {}
            for kb in range(nkb):
                run_waits = []
                for hp in range(2):
                    sslot, sissue = st_slot[(hp, kb)]
                    pr = hp * 3 + kb % 3

                    def exp_op(e, pr=pr, sslot=sslot):
                        return e.activation(p_sb6[pr][:], pst[sslot][:, 0:512],
                                            AF.Exp)
                    w = [(S["pe"], st_cnt[(hp, kb)])]
                    prev_pv = pv_hist.get((hp, g, kb - 3)) if kb >= 3 else \
                        pv_hist.get((hp, g - 1, "last"))
                    if prev_pv:
                        w.append((S["pe"], prev_pv))
                    c_exp = P.op("scalar", S["act"], w, exp_op)
                    exp_hist[(hp, g, kb)] = c_exp
                    exp_by_issue[sissue] = c_exp
                    if hp == 0:
                        run_waits.append((S["act"], c_exp))
                    rdy = ("act", c_exp)
                    r = kb - 4 * g
                    if r >= 0:
                        c_msk = P.op("vector", S["dve"], [(S["act"], c_exp)],
                                     lambda e, pr=pr, r=r: e.tensor_tensor(
                                         p_sb6[pr][:], p_sb6[pr][:],
                                         C.diag_mask[r], ALU.mult))
                        rdy = ("dve", c_msk)
                    msk_or_exp[(hp, kb)] = rdy
                if kb >= 1:
                    if kb - 1 == 0:
                        run_waits += [(S["act"], c_orel[0]), (S["act"], c_orel[1])]
                    for hp in range(2):
                        rdy = msk_or_exp[(hp, kb - 1)]
                        if rdy[0] == "dve":
                            run_waits.append((S["dve"], rdy[1]))
                    first = True
                    for hp in range(2):
                        emit_pv(kb - 1, hp, waits=run_waits if first else [])
                        first = False
                    for hp in range(2):
                        if kb + 1 < nkb:
                            emit_st(g, hp, kb + 1, waits=[])
                else:
                    for hp in range(2):
                        if kb + 1 < nkb:
                            emit_st(g, hp, kb + 1,
                                    waits=run_waits if hp == 0 else [])
                need = len(p1_thunks) - (nkb - 1 - kb - 1) * 2
                if need > 0:
                    drain_p1(need)
            drain_p1(len(p1_thunks))
            for hp in range(2):
                emit_pv(nkb - 1, hp)
            for hp in range(2):
                exp_hist[(hp, g, "last")] = exp_hist[(hp, g, nkb - 1)]
                pv_hist[(hp, g, "last")] = pv_hist[(hp, g, nkb - 1)]
            norm_tail(g - 1, 0)
            norm_tail(g - 1, 1)
        pvA = pv_hist[(0, NG - 1, 4 * NG - 1)]
        pvB = pv_hist[(1, NG - 1, 4 * NG - 1)]

        def pair_tail(j=j, pvA=pvA, pvB=pvB):
            pv_hist[(0, NG - 1, 4 * NG - 1)] = pvA
            pv_hist[(1, NG - 1, 4 * NG - 1)] = pvB
            release_O(NG - 1, 0)
            release_O(NG - 1, 1)
            norm_tail(NG - 1, 0, j=j)
            norm_tail(NG - 1, 1, j=j)
        pend_tail[0] = pair_tail
    pend_tail[0]()
    P.flush("attn_B" + ut)

    c_wo_pairs = []
    for half in range(2):
        pr, _xw = wc.load(
            P, (tag, "wo", half),
            wq_buf[half][:].rearrange("p (c o) -> p c o", o=D),
            wq_buf[half][:],
            W["wo"].rearrange("(c p) o -> p c o", p=128)[
                :, half * 4:(half + 1) * 4, :],
            4 * D, [])
        c_wo_pairs.append(pr)
    c_ev_prev = 0
    c_ev3 = {}
    c_mmC = {}
    c_wb_store = 0
    NT2 = 256
    for tt in range(T // NT2):
        tsl = slice(tt * NT2, (tt + 1) * NT2)
        c_x = P.dma("sync", S["dx"], x_t[:].rearrange("p (c t) -> p c t", t=NT2),
                    fm3(xin)[:, :, tsl],
                    waits=[(S["dve"], c_ev_prev), (S["dx"], c_wb_store)])
        ab = tt % 2
        c_a = P.dma("sync", S["dx"],
                    xn_t[ab][:, 0:KC * NT2].rearrange("p (c t) -> p c t", t=NT2),
                    fm3(W["af"])[:, :, tsl],
                    waits=[(S["pe"], c_mmC.get(tt - 2, 0))])
        for oc in range(KC):
            pi = oc % 2

            def wo_mm(e, oc=oc, pi=pi, ab=ab):
                for c in range(KC):
                    last = e.matmul(
                        pp[pi][:, 0:NT2],
                        wq_buf[c // 4][:, (c % 4) * D + oc * 128:
                                       (c % 4) * D + (oc + 1) * 128],
                        xn_t[ab][:, c * NT2:(c + 1) * NT2],
                        start=(c == 0), stop=(c == KC - 1))
                return last
            w = c_wo_pairs + [(S["dx"], c_a)]
            prev = c_ev3.get((tt, oc - 2)) or c_ev3.get((tt - 1, oc + KC - 2))
            if prev:
                w.append((S["dve"], prev))
            c_mm = P.op("tensor", S["pe"], w, wo_mm)
            c_mmC[tt] = c_mm

            if W["has_bias"]:
                def ev(e, oc=oc, pi=pi):
                    xsl = slice(oc * NT2, (oc + 1) * NT2)
                    e.tensor_scalar(vstage[:, 0:NT2], pp[pi][:, 0:NT2],
                                    bias["bo"][:, oc:oc + 1], None, ALU.add)
                    return e.tensor_tensor(x_t[:, xsl], x_t[:, xsl],
                                           vstage[:, 0:NT2], ALU.add)
            else:
                def ev(e, oc=oc, pi=pi):
                    xsl = slice(oc * NT2, (oc + 1) * NT2)
                    return e.scalar_tensor_tensor(x_t[:, xsl], pp[pi][:, 0:NT2],
                                                  1.0, x_t[:, xsl],
                                                  ALU.mult, ALU.add)
            c_e = P.op("vector", S["dve"], [(S["pe"], c_mm), (S["dx"], c_x)], ev)
            c_ev3[(tt, oc)] = c_e
        c_ev_prev = c_ev3[(tt, KC - 1)]
        c_wb_store = P.dma("sync", S["dx"], fm3(xout)[:, :, tsl],
                           x_t[:].rearrange("p (c t) -> p c t", t=NT2),
                           waits=[(S["dve"], c_ev_prev)])
    tail.append((S["dx"], c_wb_store))
    P.flush("attn_C" + ut)


def build_attn_program(has_bias=True, debug=False):
    nc = bass.Bass("TRN2", target_bir_lowering=False, debug=False)
    xin = nc.dram_tensor("x", [D, T], F32, kind="ExternalInput").ap()
    xout = nc.dram_tensor("y", [D, T], F32, kind="ExternalOutput").ap()
    qf_kind = "ExternalOutput" if debug else "Internal"
    W = {
        "has_bias": has_bias,
        "qf": nc.dram_tensor("qf", [D, T], BF16, kind=qf_kind).ap(),
        "af": nc.dram_tensor("af", [D, T], BF16, kind=qf_kind).ap(),
        "wo": nc.dram_tensor("wo", [D, D], F32, kind="ExternalInput").ap(),
    }
    if has_bias:
        wq = nc.dram_tensor("wq", [D, D], F32, kind="ExternalInput").ap()
        wk = nc.dram_tensor("wk", [D, D], F32, kind="ExternalInput").ap()
        wv = nc.dram_tensor("wv", [D, D], F32, kind="ExternalInput").ap()
        mats = [wq, wk, wv]
        W["qkv_src"] = lambda oc: (mats[oc // KC], (oc % KC) * 128)
        for n in ("bq8", "bk", "bv", "bo"):
            nm = {"bq8": "bq"}.get(n, n)
            W[n] = nc.dram_tensor(nm, [D], F32, kind="ExternalInput").ap()
    else:
        wqkv = nc.dram_tensor("wqkv", [D, 3 * D], F32, kind="ExternalInput").ap()
        W["qkv_src"] = lambda oc: (wqkv, oc * 128)
    W["wc"] = WCache(nc, "wcache", 32768 + 64)
    W["tag"] = "a0"
    with ExitStack() as stack:
        P = Prog(nc, stack)
        C = Consts(nc, P, stack, need_attn=True)
        tail = []
        emit_attn(nc, P, C, stack, xin, xout, W, tail, dbg=debug)
        with nc.Block("fin") as blk:
            @blk.sync
            def _(eng):
                for sem, cnt in tail:
                    eng.wait_ge(sem.h, cnt)
    return nc


def _prep_all(inputs):
    c = lambda a: np.ascontiguousarray(a, dtype=np.float32)
    w0 = {
        "sg": c(inputs["L0_ffn_shared_gate"]),
        "su": c(inputs["L0_ffn_shared_up"]),
        "sd": c(inputs["L0_ffn_shared_down"]),
        "rg": c(inputs["L0_ffn_routed_gate"]),
        "ru": c(inputs["L0_ffn_routed_up"]),
        "rd": c(inputs["L0_ffn_routed_down"]),
        "wr": c(inputs["L0_ffn_Wr"]),
        "rb": c(inputs["L0_ffn_routing_bias"]),
    }
    a0 = {
        "wq": c(inputs["L0_attn_Wq"]), "wk": c(inputs["L0_attn_Wk"]),
        "wv": c(inputs["L0_attn_Wv"]), "wo": c(inputs["L0_attn_Wo"]),
        "bq": c(inputs["L0_attn_bq"]), "bk": c(inputs["L0_attn_bk"]),
        "bv": c(inputs["L0_attn_bv"]), "bo": c(inputs["L0_attn_bo"]),
    }
    a1 = {
        "wqkv": c(inputs["L1_attn_W_qkv"].T),
        "wo": c(inputs["L1_attn_W_o"].T),
    }
    w1 = {
        "w1": c(inputs["L1_ffn_w1_w"].T), "w2": c(inputs["L1_ffn_w2_w"].T),
        "h1": c(inputs["L1_ffn_h1_w"].T), "h2": c(inputs["L1_ffn_h2_w"].T),
        "w3": c(inputs["L1_ffn_w3_w"].T),
        "s1": c(inputs["L1_ffn_w1_s"]), "s2": c(inputs["L1_ffn_w2_s"]),
        "sh1": c(inputs["L1_ffn_h1_s"]), "sh2": c(inputs["L1_ffn_h2_s"]),
        "s3": c(inputs["L1_ffn_w3_s"]),
    }
    return w0, a0, a1, w1


_PROG_CACHE = {}


def _prog(name):
    if name not in _PROG_CACHE:
        if name == "ffn0":
            _PROG_CACHE[name] = build_ffn0_program()
        elif name == "ffn1":
            _PROG_CACHE[name] = build_ffn1_program()
        elif name == "attn0":
            _PROG_CACHE[name] = build_attn_program(has_bias=True)
        elif name == "attn1":
            _PROG_CACHE[name] = build_attn_program(has_bias=False)
    return _PROG_CACHE[name]


def build_fused_program():
    nc = bass.Bass("TRN2", target_bir_lowering=False, debug=False)
    xin = nc.dram_tensor("x", [D, T], F32, kind="ExternalInput").ap()
    y = nc.dram_tensor("y", [D, T], F32, kind="ExternalOutput").ap()
    qf = nc.dram_tensor("qf", [D, T], BF16, kind="Internal").ap()
    af = nc.dram_tensor("af", [D, T], BF16, kind="Internal").ap()
    W0 = {
        "sg": nc.dram_tensor("sg", [D, HID0], F32, kind="ExternalInput").ap(),
        "su": nc.dram_tensor("su", [D, HID0], F32, kind="ExternalInput").ap(),
        "sd": nc.dram_tensor("sd", [HID0, D], F32, kind="ExternalInput").ap(),
        "rg": nc.dram_tensor("rg", [NEXP, D, HID0], F32, kind="ExternalInput").ap(),
        "ru": nc.dram_tensor("ru", [NEXP, D, HID0], F32, kind="ExternalInput").ap(),
        "rd": nc.dram_tensor("rd", [NEXP, HID0, D], F32, kind="ExternalInput").ap(),
        "wr": nc.dram_tensor("wr", [D, NEXP], F32, kind="ExternalInput").ap(),
        "rb": nc.dram_tensor("rb", [NEXP], F32, kind="ExternalInput").ap(),
    }
    wq = nc.dram_tensor("wq", [D, D], F32, kind="ExternalInput").ap()
    wk = nc.dram_tensor("wk", [D, D], F32, kind="ExternalInput").ap()
    wv = nc.dram_tensor("wv", [D, D], F32, kind="ExternalInput").ap()
    mats0 = [wq, wk, wv]
    A0 = {
        "has_bias": True, "qf": qf, "af": af,
        "qkv_src": lambda oc: (mats0[oc // KC], (oc % KC) * 128),
        "wo": nc.dram_tensor("wo0", [D, D], F32, kind="ExternalInput").ap(),
        "bq8": nc.dram_tensor("bq", [D], F32, kind="ExternalInput").ap(),
        "bk": nc.dram_tensor("bk", [D], F32, kind="ExternalInput").ap(),
        "bv": nc.dram_tensor("bv", [D], F32, kind="ExternalInput").ap(),
        "bo": nc.dram_tensor("bo", [D], F32, kind="ExternalInput").ap(),
    }
    wqkv = nc.dram_tensor("wqkv", [D, 3 * D], F32, kind="ExternalInput").ap()
    A1 = {
        "has_bias": False, "qf": qf, "af": af,
        "qkv_src": lambda oc: (wqkv, oc * 128),
        "wo": nc.dram_tensor("wo1", [D, D], F32, kind="ExternalInput").ap(),
    }
    W1 = {
        "w1": nc.dram_tensor("w1", [D, HID1], F32, kind="ExternalInput").ap(),
        "w2": nc.dram_tensor("w2", [D, HID1], F32, kind="ExternalInput").ap(),
        "h1": nc.dram_tensor("h1", [HID1, HID1], F32, kind="ExternalInput").ap(),
        "h2": nc.dram_tensor("h2", [HID1, HID1], F32, kind="ExternalInput").ap(),
        "w3": nc.dram_tensor("w3", [HID1, D], F32, kind="ExternalInput").ap(),
        "s1": nc.dram_tensor("s1", [HID1], F32, kind="ExternalInput").ap(),
        "s2": nc.dram_tensor("s2", [HID1], F32, kind="ExternalInput").ap(),
        "sh1": nc.dram_tensor("sh1", [HID1], F32, kind="ExternalInput").ap(),
        "sh2": nc.dram_tensor("sh2", [HID1], F32, kind="ExternalInput").ap(),
        "s3": nc.dram_tensor("s3", [D], F32, kind="ExternalInput").ap(),
    }
    wcache = WCache(nc, "wcache", 966656 + 256)
    for wdict, tg in ((W0, "w0"), (A0, "a0"), (A1, "a1"), (W1, "w1")):
        wdict["wc"] = wcache
        wdict["tag"] = tg
    with ExitStack() as ostack:
        P = Prog(nc, ostack)
        C = Consts(nc, P, ostack, need_attn=True)
        tail = []
        seq = [
            ("ffn0", emit_ffn0, W0, xin, y),
            ("attn0", emit_attn, A0, y, y),
            ("ffn0b", emit_ffn0, W0, y, y),
            ("ffn1", emit_ffn1, W1, y, y),
            ("attn1", emit_attn, A1, y, y),
            ("ffn1b", emit_ffn1, W1, y, y),
        ]
        for name, fn, wts, si, so in seq:
            P.wait("gpsimd", P.s["dx"], P.s["dx"].n)
            with ExitStack() as stack:
                fn(nc, P, C, stack, si, so, wts, tail)
        with nc.Block("fin") as blk:
            @blk.sync
            def _(eng):
                for sem, cnt in tail:
                    eng.wait_ge(sem.h, cnt)
    return nc


def _fused_in_map(inputs):
    w0, a0, a1, w1 = _prep_all(inputs)
    m = {}
    m.update(w0)
    m.update({"wq": a0["wq"], "wk": a0["wk"], "wv": a0["wv"], "wo0": a0["wo"],
              "bq": a0["bq"], "bk": a0["bk"], "bv": a0["bv"], "bo": a0["bo"]})
    m.update({"wqkv": a1["wqkv"], "wo1": a1["wo"]})
    m.update(w1)
    return m


def kernel(**inputs):
    ncores = 8
    x = np.asarray(inputs["L0_attn_x"], dtype=np.float32)
    wmap = _fused_in_map(inputs)
    nc = build_fused_program()
    in_maps = [dict(wmap, x=np.ascontiguousarray(x[b].T)) for b in range(ncores)]
    res = run_bass_kernel_spmd(nc, in_maps, core_ids=list(range(ncores)))
    return np.stack([np.asarray(res.results[b]["y"]).T for b in range(ncores)],
                    axis=0)


def kernel_unfused(**inputs):
    ncores = 8
    x = np.asarray(inputs["L0_attn_x"], dtype=np.float32)
    cur = [np.ascontiguousarray(x[b].T) for b in range(ncores)]
    w0, a0, a1, w1 = _prep_all(inputs)
    seq = [("ffn0", w0), ("attn0", a0), ("ffn0", w0),
           ("ffn1", w1), ("attn1", a1), ("ffn1", w1)]
    for name, wts in seq:
        nc = _prog(name)
        in_maps = [dict(wts, x=cur[b]) for b in range(ncores)]
        res = run_bass_kernel_spmd(nc, in_maps, core_ids=list(range(ncores)))
        cur = [np.asarray(res.results[b]["y"]) for b in range(ncores)]
    return np.stack([cur[b].T for b in range(ncores)], axis=0)


# version 15 (final)
# speedup vs baseline: 1.0328x; 1.0328x over previous
import numpy as np
from contextlib import ExitStack

import concourse.bass as bass
import concourse.mybir as mybir
from concourse.bass_utils import run_bass_kernel_spmd

F32 = mybir.dt.float32
BF16 = mybir.dt.bfloat16
AF = mybir.ActivationFunctionType
ALU = mybir.AluOpType
AX = mybir.AxisListType

D = 1024
KC = 8
T = 4096
NH = 16
DKH = 64
HID0 = 2816
HC0 = 22
NEXP = 7
HID1 = 4096
HC1 = 32
TQ = 1024
NQ = T // TQ
LN_EPS = 1e-5
ATT_SCALE = 0.125

ENGINES = ("tensor", "vector", "scalar", "gpsimd", "sync")
_UNIQ = [0]


def _tag():
    _UNIQ[0] += 1
    return f"_u{_UNIQ[0]}"


class Sem:
    def __init__(self, nc, stack, name):
        self.h = stack.enter_context(nc.semaphore(name))
        self.n = 0

    def reserve(self, k):
        self.n += k
        return self.n


class Prog:
    def __init__(self, nc, stack):
        self.nc = nc
        self.streams = {k: [] for k in ENGINES}
        self.s = {k: Sem(nc, stack, "s_" + k) for k in
                  ("pe", "act", "dve", "pool", "dw", "dx", "dq")}

    def on(self, eng, fn):
        self.streams[eng].append(fn)

    def op(self, eng, sem, waits, fn):
        tgt = sem.reserve(1)

        def body(e, fn=fn, waits=tuple(waits)):
            for ws, wc in waits:
                if wc > 0:
                    e.wait_ge(ws.h, wc)
            fn(e).then_inc(sem.h, 1)
        self.on(eng, body)
        return tgt

    def wait(self, eng, sem, count):
        if count > 0:
            self.on(eng, lambda e, c=count: e.wait_ge(sem.h, c))

    def dma(self, queue, sem, dst, src, waits=(), **kw):
        tgt = sem.reserve(16)

        def body(e, d=dst, s=src, w=tuple(waits), kw=kw):
            for ws, wc in w:
                if wc > 0:
                    e.wait_ge(ws.h, wc)
            e.dma_start(d, s, **kw).then_inc(sem.h, 16)
        self.on(queue, body)
        return tgt

    def flush(self, name=None):
        live = {k: v for k, v in self.streams.items() if v}
        self.streams = {k: [] for k in ENGINES}
        if not live:
            return
        with self.nc.Block(name) as blk:
            for eng_name, fns in live.items():
                def body(eng, fns=fns):
                    for f in fns:
                        f(eng)
                getattr(blk, eng_name)(body)


class WCache:

    def __init__(self, nc, name, total_cols):
        self.t = nc.dram_tensor(name, [128, total_cols], BF16, kind="Internal").ap()
        self.off = {}
        self.built = {}
        self.cur = 0
        self.total = total_cols

    def load(self, P, key, dst3d, dst2d, src3d, cols, waits):
        if key in self.built:
            off = self.off[key]
            c = P.dma("gpsimd", P.s["dw"], dst2d, self.t[:, off:off + cols],
                      waits=waits)
            return (P.s["dw"], c), None
        off = self.cur
        assert off + cols <= self.total, (key, off, cols, self.total)
        self.off[key] = off
        self.cur += cols
        c1 = P.dma("gpsimd", P.s["dw"], dst3d, src3d, waits=waits)
        c2 = P.dma("sync", P.s["dx"], self.t[:, off:off + cols], dst2d,
                   waits=[(P.s["dw"], c1)])
        self.built[key] = c2
        return (P.s["dw"], c1), (P.s["dx"], c2)


def fm3(ap2d, p=128):
    return ap2d.rearrange("(c p) t -> p c t", p=p)


class Consts:
    def __init__(self, nc, P, stack, need_attn=True):
        sb = lambda n, shp, dt: stack.enter_context(nc.sbuf_tensor(n, shp, dt))
        self.ident = sb("c_ident", [128, 128], BF16)
        self.ones_col = sb("c_ones_col", [128, 1], BF16)
        self.ones_row = sb("c_ones_row", [1, 128], BF16)
        self.ones_row32 = sb("c_ones_row32", [1, 128], F32)
        iota = sb("c_iota", [128, 128], F32)
        self.need_attn = need_attn
        if need_attn:
            self.dmask2 = [sb(f"c_dmask2_{i}", [128, 1024], BF16) for i in range(2)]
            self.diag_mask = [self.dmask2[r // 2][:, (r % 2) * 512:(r % 2 + 1) * 512]
                              for r in range(4)]
            iota_q = sb("c_iota_q", [128, 512], F32)

        def g(eng):
            eng.iota(iota[:], [[-1, 128]], base=0, channel_multiplier=1,
                     allow_small_or_imprecise_dtypes=True)
            eng.memset(self.ones_col[:], 1.0)
            eng.memset(self.ones_row[:], 1.0)
            eng.memset(self.ones_row32[:], 1.0)
            if need_attn:
                eng.iota(iota_q[:], [[1, 512]], base=0, channel_multiplier=-1,
                         allow_small_or_imprecise_dtypes=True)
        P.on("gpsimd", g)
        P.flush("consts_a")

        def v(eng):
            eng.tensor_scalar(self.ident[:], iota[:], 0.0, None, ALU.is_equal)
            if need_attn:
                for r in range(4):
                    eng.tensor_scalar(self.diag_mask[r][:], iota_q[:],
                                      float(128 * r), None, ALU.is_ge)
        P.on("vector", v)
        P.flush("consts_b")


def emit_ln(P, C, lnp, x_q, xn_q, tq, x_wait, pe_stat_waits=(), pe_bc_waits=(),
            copy_waits=(), pre_cast=False):
    S = P.s
    if pre_cast:
        c_copy = P.op("vector", S["dve"], [x_wait] + list(copy_waits),
                      lambda e: e.tensor_tensor(lnp["sq"][:], xn_q[:], xn_q[:],
                                                ALU.mult))
    else:
        c_copy = P.op("vector", S["dve"], [x_wait] + list(copy_waits), lambda e: (
            e.tensor_copy(xn_q[:], x_q[:]),
            e.tensor_tensor(lnp["sq"][:], xn_q[:], xn_q[:], ALU.mult))[-1])
    c_cb = 0
    c_done = 0
    c_bc = 0
    for it in range(tq // 512):
        t0, t1 = it * 512, (it + 1) * 512
        r0, r1 = 0, 512

        def stats_mms(e, t0=t0, t1=t1):
            for c in range(KC):
                last = e.matmul(lnp["s1ps"], C.ones_col[:],
                                xn_q[:, c * tq + t0: c * tq + t1],
                                start=(c == 0), stop=(c == KC - 1))
            for c in range(KC):
                last = e.matmul(lnp["s2ps"], C.ones_col[:],
                                lnp["sq"][:, c * tq + t0: c * tq + t1],
                                start=(c == 0), stop=(c == KC - 1))
            return last
        w = [(S["dve"], c_copy)] + (list(pe_stat_waits) if it == 0 else [])
        if it > 0:
            w.append((S["dve"], c_var))
        c_mm = P.op("tensor", S["pe"], w, stats_mms)

        def dve_stats(e, r0=r0, r1=r1):
            e.tensor_scalar(lnp["m32"][0:1, r0:r1], lnp["s1ps"], 1.0 / D, None, ALU.mult)
            e.tensor_tensor(lnp["var"][0:1, r0:r1], lnp["m32"][0:1, r0:r1],
                            lnp["m32"][0:1, r0:r1], ALU.mult)
            e.scalar_tensor_tensor(lnp["var"][0:1, r0:r1], lnp["s2ps"], 1.0 / D,
                                   lnp["var"][0:1, r0:r1], ALU.mult, ALU.subtract)
            return e.tensor_scalar(lnp["var"][0:1, r0:r1], lnp["var"][0:1, r0:r1],
                                   LN_EPS, None, ALU.add)
        c_var = P.op("vector", S["dve"], [(S["pe"], c_mm)], dve_stats)

        c_sd = P.op("scalar", S["act"], [(S["dve"], c_var)],
                    lambda e, r0=r0, r1=r1: e.activation(
                        lnp["sd"][0:1, r0:r1], lnp["var"][0:1, r0:r1], AF.Sqrt))

        def dve_rows(e, r0=r0, r1=r1):
            e.tensor_scalar(lnp["nm_bf"][0:1, r0:r1], lnp["m32"][0:1, r0:r1],
                            -1.0, None, ALU.mult)
            e.reciprocal(lnp["r32"][0:1, r0:r1], lnp["sd"][0:1, r0:r1])
            return e.tensor_scalar(lnp["r_bf"][0:1, r0:r1], lnp["r32"][0:1, r0:r1],
                                   1.0, None, ALU.mult)
        c_rows = P.op("vector", S["dve"], [(S["act"], c_sd), (S["pe"], c_bc)],
                      dve_rows)

        def bcast_mms(e):
            e.matmul(lnp["mbps"], C.ones_row[:], lnp["nm_bf"][0:1, 0:512],
                     start=True, stop=True)
            return e.matmul(lnp["rbps"], C.ones_row[:], lnp["r_bf"][0:1, 0:512],
                            start=True, stop=True)
        w = [(S["dve"], c_rows)]
        if it == 0:
            w += list(pe_bc_waits)
        else:
            w.append((S["act"], c_cb))
        c_bc = P.op("tensor", S["pe"], w, bcast_mms)

        def act_cb(e):
            e.activation(lnp["mb"][:, 0:512], lnp["mbps"], AF.Copy)
            return e.activation(lnp["rb"][:, 0:512], lnp["rbps"], AF.Copy)
        c_cb = P.op("scalar", S["act"], [(S["pe"], c_bc)], act_cb)

        def dve_norm(e, t0=t0, t1=t1):
            for c in range(KC):
                sl = slice(c * tq + t0, c * tq + t1)
                e.tensor_tensor(xn_q[:, sl], xn_q[:, sl], lnp["mb"][:, 0:512], ALU.add)
                last = e.tensor_tensor(xn_q[:, sl], xn_q[:, sl], lnp["rb"][:, 0:512], ALU.mult)
            return last
        c_done = P.op("vector", S["dve"], [(S["act"], c_cb)], dve_norm)
    return {"xn": c_done, "act_bc": c_cb, "var": c_var, "copy": c_copy}


def make_lnp(nc, stack, tag, tq, sq_alias, pp, mb_alias=None, rb_alias=None,
             nm_alias=None, r_alias=None, m32_alias=None, var_alias=None):
    sb = lambda n, shp, dt: stack.enter_context(nc.sbuf_tensor(n + tag, shp, dt))
    d = {
        "sq": sq_alias,
        "m32": m32_alias if m32_alias is not None else sb("ln_m32", [1, 512], F32),
        "var": var_alias if var_alias is not None else sb("ln_var", [1, 512], F32),
        "nm_bf": nm_alias if nm_alias is not None else sb("ln_nm", [1, 512], BF16),
        "r_bf": r_alias if r_alias is not None else sb("ln_rbf", [1, 512], BF16),
        "mb": mb_alias if mb_alias is not None else sb("ln_mb", [128, 512], BF16),
        "rb": rb_alias if rb_alias is not None else sb("ln_rb", [128, 512], BF16),
        "s1ps": pp["s1"], "s2ps": pp["s2"],
        "mbps": pp["mb"], "rbps": pp["rb"],
    }
    d["sd"] = d["var"]
    d["r32"] = d["m32"]
    return d


def emit_ffn0(nc, P, C, stack, xin, xout, W, tail, dbg=None):
    S = P.s
    ut = _tag()
    sb = lambda n, shp, dt: stack.enter_context(nc.sbuf_tensor(n + ut, shp, dt))
    x_q = sb("f0_x", [128, KC * TQ], F32)
    xn_q = sb("f0_xn", [128, KC * TQ], BF16)
    g_sb = sb("f0_g", [128, HC0 * TQ], BF16)
    wg_buf = [sb(f"f0_wg{i}", [128, KC * 512], BF16) for i in range(2)]
    wu_buf = [sb(f"f0_wu{i}", [128, KC * 512], BF16) for i in range(2)]
    wd_buf = [sb(f"f0_wd{i}", [128, HC0 * 256], BF16) for i in range(2)]
    t1_buf = [sb(f"f0_t1{i}", [128, 512], BF16) for i in range(2)]
    td_buf = [sb(f"f0_td{i}", [128, 512], BF16) for i in range(2)]
    wrows = sb("f0_wrows", [1, NEXP * TQ], BF16)
    wb_sb = sb("f0_wb", [128, TQ], BF16)
    wr_sb = sb("f0_wr", [128, KC * NEXP], BF16)
    rbb_sb = sb("f0_rbb", [128, NEXP], F32)
    rb_row = sb("f0_rbrow", [1, NEXP], F32)
    rt = {n: sb("f0_" + n, [128, NEXP], F32)
          for n in ("lg", "probs", "ismax", "probs2", "keep", "wun")}
    wfin = sb("f0_wfin", [128, NEXP], BF16)
    rt1 = {n: sb("f0_" + n, [128, 1], F32) for n in ("m1", "m2", "den", "invd")}

    ps = lambda n, shp, dt: stack.enter_context(nc.psum_tensor(n + ut, shp, dt))
    pg = [ps("f0_pg0", [128, 512], F32), ps("f0_pg1", [128, 512], F32)]
    pu = [ps("f0_pu0", [128, 512], F32), ps("f0_pu1", [128, 512], F32)]
    pd = [ps("f0_pd0", [128, 512], F32), ps("f0_pd1", [128, 512], F32)]
    pmisc = ps("f0_pm", [128, 512], F32)

    lnp = make_lnp(nc, stack, "f0" + ut, TQ, g_sb[:, 0:KC * TQ],
                   {"s1": pd[0][0:1, 0:512], "s2": pd[1][0:1, 0:512],
                    "mb": pg[0][:, 0:512], "rb": pg[1][:, 0:512]})

    c_wr = P.dma("gpsimd", S["dw"], wr_sb[:].rearrange("p (c o) -> p c o", o=NEXP),
                 W["wr"].rearrange("(c p) o -> p c o", p=128))
    c_rb = P.dma("sync", S["dx"], rb_row[0:1, :],
                 W["rb"].rearrange("(o e) -> o e", o=1))
    c_rbbmm = P.op("tensor", S["pe"], [(S["dx"], c_rb)],
                   lambda e: e.matmul(pmisc[:, 0:NEXP], C.ones_row32[:],
                                      rb_row[0:1, :], start=True, stop=True))
    c_rbb = P.op("vector", S["dve"], [(S["pe"], c_rbbmm)],
                 lambda e: e.tensor_copy(rbb_sb[:], pmisc[:, 0:NEXP]))

    wc = W["wc"]
    tag = W["tag"]
    slot_x_g = [[], []]; slot_x_u = [[], []]; slot_x_d = [[], []]
    cons_wg = [0, 0]; cons_wu = [0, 0]; cons_wd = [0, 0]
    c_q_done = 0
    c_last_silu = 0
    c_last_gmul = 0
    c_pmisc_free = c_rbb
    c_pmisc_free_sem = "dve"
    c_wb_store = 0
    last_evac = 0
    dbg_pend = {"xn": 0, "w": 0, "g": 0}
    CH = [(i * 4, min(4, HC0 - i * 4)) for i in range((HC0 + 3) // 4)]

    for q in range(NQ):
        qsl = slice(q * TQ, (q + 1) * TQ)
        c_x = P.dma("sync", S["dx"], x_q[:].rearrange("p (c t) -> p c t", t=TQ),
                    fm3(xin)[:, :, qsl],
                    waits=[(S["dve"], c_q_done), (S["dx"], c_wb_store)])

        ln = emit_ln(P, C, lnp, x_q, xn_q, TQ, (S["dx"], c_x),
                     pe_stat_waits=[(S["dve"], c_q_done)],
                     pe_bc_waits=[(S["act"], c_last_silu)],
                     copy_waits=[(S["dq"], dbg_pend["xn"])])
        c_xn = ln["xn"]

        c_lg_last = 0
        c_rtcopy = 0
        c_tr = 0
        for qt in range(TQ // 128):
            def route_mm(e, qt=qt):
                for c in range(KC):
                    last = e.matmul(
                        pmisc[:, 0:NEXP],
                        xn_q[:, c * TQ + qt * 128: c * TQ + (qt + 1) * 128],
                        wr_sb[:, c * NEXP:(c + 1) * NEXP],
                        start=(c == 0), stop=(c == KC - 1))
                return last
            w = [(S["dve"], c_xn), (S["dw"], c_wr)]
            w.append((S[c_pmisc_free_sem], c_pmisc_free))
            if dbg is not None and "rt" in dbg:
                w.append((S["dq"], dbg_pend["w"]))
            c_lmm = P.op("tensor", S["pe"], w, route_mm)

            c_lg_last = P.op("vector", S["dve"], [(S["pe"], c_lmm), (S["dve"], c_rbb)],
                             lambda e: e.tensor_tensor(rt["lg"][:], pmisc[:, 0:NEXP],
                                                       rbb_sb[:], ALU.mult))
            c_pmisc_free, c_pmisc_free_sem = c_lg_last, "dve"

            c_sig = P.op("scalar", S["act"], [(S["dve"], c_lg_last)],
                         lambda e: e.activation(rt["probs"][:], rt["lg"][:], AF.Sigmoid))

            cc = P.op("vector", S["dve"], [(S["act"], c_sig), (S["pe"], c_tr)],
                      lambda e: e.tensor_reduce(rt1["m1"][:], rt["probs"][:],
                                                AX.X, ALU.max))
            cc = P.op("vector", S["dve"], [(S["dve"], cc)],
                      lambda e: e.tensor_scalar(rt["ismax"][:], rt["probs"][:],
                                                rt1["m1"][:, 0:1], None, ALU.is_ge))
            cc = P.op("vector", S["dve"], [(S["dve"], cc)],
                      lambda e: e.scalar_tensor_tensor(rt["probs2"][:], rt["ismax"][:],
                                                       -2.0, rt["probs"][:],
                                                       ALU.mult, ALU.add))
            cc = P.op("vector", S["dve"], [(S["dve"], cc)],
                      lambda e: e.tensor_reduce(rt1["m2"][:], rt["probs2"][:],
                                                AX.X, ALU.max))
            cc = P.op("vector", S["dve"], [(S["dve"], cc)],
                      lambda e: (e.tensor_scalar(rt["keep"][:], rt["probs"][:],
                                                 rt1["m2"][:, 0:1], None, ALU.is_ge),
                                 e.tensor_tensor(rt1["den"][:], rt1["m1"][:],
                                                 rt1["m2"][:], ALU.add))[-1])
            cc = P.op("vector", S["dve"], [(S["dve"], cc)],
                      lambda e: (e.tensor_tensor(rt["wun"][:], rt["probs"][:],
                                                 rt["keep"][:], ALU.mult),
                                 e.reciprocal(rt1["invd"][:], rt1["den"][:]))[-1])
            c_w = P.op("vector", S["dve"], [(S["dve"], cc)],
                       lambda e: e.tensor_scalar(wfin[:], rt["wun"][:],
                                                 rt1["invd"][:, 0:1], 0.5,
                                                 ALU.mult, ALU.mult))
            if dbg is not None and "rt" in dbg and q == 0 and qt == 0:
                for nm, t in [("lg", rt["lg"]), ("probs", rt["probs"]),
                              ("rbb", rbb_sb), ("keep", rt["keep"]),
                              ("den", rt1["den"])]:
                    dbg_pend["w"] = P.dma("scalar", S["dq"], dbg["rt_" + nm],
                                          t[:, :], waits=[(S["dve"], c_w)])
                    tail.append((S["dq"], dbg_pend["w"]))
                dbg_pend["w"] = P.dma("scalar", S["dq"], dbg["rt_wfin"],
                                      wfin[:, :], waits=[(S["dve"], c_w)])
                tail.append((S["dq"], dbg_pend["w"]))

            c_rtcopy_grp = 0
            for g0, g1 in ((0, 4), (4, NEXP)):
                def row_mms(e, g0=g0, g1=g1):
                    for ex in range(g0, g1):
                        last = e.matmul(pmisc[0:1, (ex - g0) * 128:(ex - g0 + 1) * 128],
                                        wfin[:, ex:ex + 1], C.ident[:],
                                        start=True, stop=True)
                    return last
                wts = [(S["dve"], c_w)]
                if g0 == 0:
                    wts += [(S[c_pmisc_free_sem], c_pmisc_free),
                            (S["act"], c_rtcopy)]
                else:
                    wts.append((S["act"], c_rtcopy_grp))
                c_tr = P.op("tensor", S["pe"], wts, row_mms)

                def row_copy(e, g0=g0, g1=g1, qt=qt):
                    dst = wrows[0:1, :].rearrange("o (x t) -> o x t", t=TQ)[
                        :, g0:g1, qt * 128: (qt + 1) * 128]
                    src = pmisc[0:1, 0:(g1 - g0) * 128].rearrange(
                        "o (x t) -> o x t", t=128)
                    return e.activation(dst, src, AF.Copy)
                wts = [(S["pe"], c_tr)]
                if g0 == 0 and qt == 0:
                    wts.append((S["dq"], dbg_pend["w"]))
                c_rtcopy_grp = P.op("scalar", S["act"], wts, row_copy)
            c_rtcopy = c_rtcopy_grp
            c_pmisc_free, c_pmisc_free_sem = c_rtcopy, "act"
        c_routing = c_rtcopy
        if dbg is not None:
            dbg_pend["xn"] = P.dma(
                "scalar", S["dq"], fm3(dbg["xn"])[:, :, qsl],
                xn_q[:].rearrange("p (c t) -> p c t", t=TQ),
                waits=[(S["dve"], c_xn)])
            tail.append((S["dq"], dbg_pend["xn"]))
            dbg_pend["w"] = P.dma(
                "scalar", S["dq"],
                dbg["w"].rearrange("x (q t) -> x q t", t=TQ)[:, q:q + 1, :].rearrange("x q t -> q x t"),
                wrows[0:1, :].rearrange("o (x t) -> o x t", t=TQ),
                waits=[(S["act"], c_routing)])
            tail.append((S["dq"], dbg_pend["w"]))

        for ei in range(NEXP + 1):
            shared = (ei == 0)
            wg_ap = W["sg"] if shared else W["rg"][ei - 1]
            wu_ap = W["su"] if shared else W["ru"][ei - 1]
            wd_ap = W["sd"] if shared else W["rd"][ei - 1]

            c_wb_ready = 0
            if not shared:
                for tt in range(2):
                    def wb_mm(e, ei=ei, tt=tt):
                        return e.matmul(
                            pmisc[:, 0:512], C.ones_row[:],
                            wrows[0:1, (ei - 1) * TQ + tt * 512:
                                  (ei - 1) * TQ + (tt + 1) * 512],
                            start=True, stop=True)
                    w = [(S["act"], c_routing), (S[c_pmisc_free_sem], c_pmisc_free)]
                    c_mm1 = P.op("tensor", S["pe"], w, wb_mm)

                    def wb_cp(e, tt=tt):
                        return e.activation(wb_sb[:, tt * 512:(tt + 1) * 512],
                                            pmisc[:, 0:512], AF.Copy)
                    c_wb_ready = P.op("scalar", S["act"],
                                      [(S["pe"], c_mm1), (S["dve"], last_evac)],
                                      wb_cp)
                    c_pmisc_free, c_pmisc_free_sem = c_wb_ready, "act"

            silu_cnt = {}
            mult_cnt = {}
            prod_g = {}
            prod_u = {}

            def sched_gu_dma(ci, wg_ap=wg_ap, wu_ap=wu_ap, ei=ei):
                slot = ci % 2
                hc0_, nhc = CH[ci]
                cw = nhc * 128
                prod_g[ci], x1 = wc.load(
                    P, (tag, "g", ei, ci),
                    wg_buf[slot][:, 0:KC * cw].rearrange("p (c o) -> p c o", o=cw),
                    wg_buf[slot][:, 0:KC * cw],
                    wg_ap.rearrange("(c p) o -> p c o", p=128)[
                        :, :, hc0_ * 128:(hc0_ + nhc) * 128],
                    KC * cw,
                    [(S["pe"], cons_wg[slot])] + slot_x_g[slot])
                slot_x_g[slot] = [x1] if x1 else []
                prod_u[ci], x2 = wc.load(
                    P, (tag, "u", ei, ci),
                    wu_buf[slot][:, 0:KC * cw].rearrange("p (c o) -> p c o", o=cw),
                    wu_buf[slot][:, 0:KC * cw],
                    wu_ap.rearrange("(c p) o -> p c o", p=128)[
                        :, :, hc0_ * 128:(hc0_ + nhc) * 128],
                    KC * cw,
                    [(S["pe"], cons_wu[slot])] + slot_x_u[slot])
                slot_x_u[slot] = [x2] if x2 else []

            sched_gu_dma(0)
            sched_gu_dma(1)
            sidx = 0
            for ci, (hc0_, nhc) in enumerate(CH):
                slot = ci % 2
                cw = nhc * 128
                for hcl in range(nhc):
                    hc = hc0_ + hcl
                    for tt in range(2):
                        pi = sidx % 2

                        def gmm(e, slot=slot, hcl=hcl, tt=tt, pi=pi, cw=cw):
                            for c in range(KC):
                                last = e.matmul(
                                    pg[pi][:, 0:512],
                                    wg_buf[slot][:, c * cw + hcl * 128:
                                                 c * cw + (hcl + 1) * 128],
                                    xn_q[:, c * TQ + tt * 512: c * TQ + (tt + 1) * 512],
                                    start=(c == 0), stop=(c == KC - 1))
                            return last
                        w = [prod_g[ci], (S["dve"], c_xn)]
                        if sidx == 0:
                            w.append((S["dq"], dbg_pend["g"]))
                        prev_silu = silu_cnt.get(sidx - 2)
                        if prev_silu:
                            w.append((S["act"], prev_silu))
                        elif sidx < 2:
                            w.append((S["act"], c_last_silu))
                        c_pgd = P.op("tensor", S["pe"], w, gmm)

                        def umm(e, slot=slot, hcl=hcl, tt=tt, pi=pi, cw=cw):
                            for c in range(KC):
                                last = e.matmul(
                                    pu[pi][:, 0:512],
                                    wu_buf[slot][:, c * cw + hcl * 128:
                                                 c * cw + (hcl + 1) * 128],
                                    xn_q[:, c * TQ + tt * 512: c * TQ + (tt + 1) * 512],
                                    start=(c == 0), stop=(c == KC - 1))
                            return last
                        w = [prod_u[ci]]
                        prev_mult = mult_cnt.get(sidx - 2)
                        if prev_mult:
                            w.append((S["dve"], prev_mult))
                        elif sidx < 2:
                            w.append((S["dve"], c_last_gmul))
                        c_pud = P.op("tensor", S["pe"], w, umm)
                        cons_wg[slot] = c_pud
                        cons_wu[slot] = c_pud

                        rot = sidx % 2
                        w = [(S["pe"], c_pgd)]
                        if mult_cnt.get(sidx - 2):
                            w.append((S["dve"], mult_cnt[sidx - 2]))
                        elif c_last_gmul:
                            w.append((S["dve"], c_last_gmul))
                        silu_cnt[sidx] = P.op(
                            "scalar", S["act"], w,
                            lambda e, pi=pi, rot=rot: e.activation(
                                t1_buf[rot][:], pg[pi][:, 0:512], AF.Silu))

                        mult_cnt[sidx] = P.op(
                            "vector", S["dve"],
                            [(S["pe"], c_pud), (S["act"], silu_cnt[sidx])],
                            lambda e, pi=pi, rot=rot, hc=hc, tt=tt: e.tensor_tensor(
                                g_sb[:, hc * TQ + tt * 512: hc * TQ + (tt + 1) * 512],
                                t1_buf[rot][:], pu[pi][:, 0:512], ALU.mult))
                        sidx += 1
                if ci + 2 < len(CH):
                    sched_gu_dma(ci + 2)
            c_last_silu = silu_cnt[sidx - 1]
            c_last_gmul = mult_cnt[sidx - 1]
            c_g_ready = mult_cnt[sidx - 1]
            if dbg is not None and q == 0 and ei == 0:
                dbg_pend["g"] = P.dma(
                    "scalar", S["dq"],
                    dbg["g"].rearrange("(c p) t -> p c t", p=128),
                    g_sb[:].rearrange("p (c t) -> p c t", t=TQ),
                    waits=[(S["dve"], c_g_ready)])
                tail.append((S["dq"], dbg_pend["g"]))

            prod_d = {}

            def sched_d_dma(sl4, wd_ap=wd_ap, ei=ei):
                prod_d[sl4], x1 = wc.load(
                    P, (tag, "d", ei, sl4),
                    wd_buf[sl4 % 2][:].rearrange("p (c o) -> p c o", o=256),
                    wd_buf[sl4 % 2][:],
                    wd_ap.rearrange("(c p) o -> p c o", p=128)[
                        :, :, sl4 * 256:(sl4 + 1) * 256],
                    HC0 * 256,
                    [(S["pe"], cons_wd[sl4 % 2])] + slot_x_d[sl4 % 2])
                slot_x_d[sl4 % 2] = [x1] if x1 else []

            sched_d_dma(0)
            sched_d_dma(1)
            evac_cnt = {}
            didx = 0
            for oc in range(KC):
                sl4 = oc // 2
                for tt in range(2):
                    pi = didx % 2

                    def dmm(e, oc=oc, tt=tt, pi=pi, sl4=sl4):
                        for hc in range(HC0):
                            last = e.matmul(
                                pd[pi][:, 0:512],
                                wd_buf[sl4 % 2][:, hc * 256 + (oc % 2) * 128:
                                                hc * 256 + (oc % 2 + 1) * 128],
                                g_sb[:, hc * TQ + tt * 512: hc * TQ + (tt + 1) * 512],
                                start=(hc == 0), stop=(hc == HC0 - 1))
                        return last
                    w = [prod_d[sl4]]
                    if didx == 0:
                        w.append((S["dve"], c_g_ready))
                    prev_evac = evac_cnt.get(didx - 2)
                    if prev_evac:
                        w.append((S["dve"], prev_evac))
                    elif didx < 2 and last_evac:
                        w.append((S["dve"], last_evac))
                    c_pdd = P.op("tensor", S["pe"], w, dmm)
                    cons_wd[sl4 % 2] = c_pdd

                    if shared:
                        evac_cnt[didx] = P.op(
                            "vector", S["dve"], [(S["pe"], c_pdd)],
                            lambda e, oc=oc, tt=tt, pi=pi: e.scalar_tensor_tensor(
                                x_q[:, oc * TQ + tt * 512: oc * TQ + (tt + 1) * 512],
                                pd[pi][:, 0:512], 0.5,
                                x_q[:, oc * TQ + tt * 512: oc * TQ + (tt + 1) * 512],
                                ALU.mult, ALU.add))
                    else:
                        rot = didx % 2

                        def ev(e, oc=oc, tt=tt, pi=pi, rot=rot, c_wb=c_wb_ready):
                            xsl = slice(oc * TQ + tt * 512, oc * TQ + (tt + 1) * 512)
                            e.tensor_tensor(td_buf[rot][:], pd[pi][:, 0:512],
                                            wb_sb[:, tt * 512:(tt + 1) * 512], ALU.mult)
                            return e.tensor_tensor(x_q[:, xsl], x_q[:, xsl],
                                                   td_buf[rot][:], ALU.add)
                        w = [(S["pe"], c_pdd), (S["act"], c_wb_ready)]
                        evac_cnt[didx] = P.op("vector", S["dve"], w, ev)
                    didx += 1
                if oc % 2 == 1 and sl4 + 2 < 4:
                    sched_d_dma(sl4 + 2)
            last_evac = evac_cnt[didx - 1]

        c_q_done = last_evac
        c_wb_store = P.dma("sync", S["dx"],
                           fm3(xout)[:, :, qsl],
                           x_q[:].rearrange("p (c t) -> p c t", t=TQ),
                           waits=[(S["dve"], c_q_done)])
    tail.append((S["dx"], c_wb_store))
    P.flush("ffn0" + ut)


def _prep_ffn0_w(inputs):
    return {
        "sg": np.ascontiguousarray(inputs["L0_ffn_shared_gate"], np.float32),
        "su": np.ascontiguousarray(inputs["L0_ffn_shared_up"], np.float32),
        "sd": np.ascontiguousarray(inputs["L0_ffn_shared_down"], np.float32),
        "rg": np.ascontiguousarray(inputs["L0_ffn_routed_gate"], np.float32),
        "ru": np.ascontiguousarray(inputs["L0_ffn_routed_up"], np.float32),
        "rd": np.ascontiguousarray(inputs["L0_ffn_routed_down"], np.float32),
        "wr": np.ascontiguousarray(inputs["L0_ffn_Wr"], np.float32),
        "rb": np.ascontiguousarray(inputs["L0_ffn_routing_bias"], np.float32),
    }


def build_ffn0_program(debug=False):
    nc = bass.Bass("TRN2", target_bir_lowering=False, debug=False)
    xin = nc.dram_tensor("x", [D, T], F32, kind="ExternalInput").ap()
    xout = nc.dram_tensor("y", [D, T], F32, kind="ExternalOutput").ap()
    dbg = None
    if debug:
        dbg = {
            "xn": nc.dram_tensor("xn_dump", [D, T], BF16, kind="ExternalOutput").ap(),
            "w": nc.dram_tensor("w_dump", [NEXP, T], BF16, kind="ExternalOutput").ap(),
            "g": nc.dram_tensor("g_dump", [HID0, TQ], BF16, kind="ExternalOutput").ap(),
            "rt": True,
            "rt_lg": nc.dram_tensor("rt_lg", [128, NEXP], F32, kind="ExternalOutput").ap(),
            "rt_probs": nc.dram_tensor("rt_probs", [128, NEXP], F32, kind="ExternalOutput").ap(),
            "rt_rbb": nc.dram_tensor("rt_rbb", [128, NEXP], F32, kind="ExternalOutput").ap(),
            "rt_keep": nc.dram_tensor("rt_keep", [128, NEXP], F32, kind="ExternalOutput").ap(),
            "rt_den": nc.dram_tensor("rt_den", [128, 1], F32, kind="ExternalOutput").ap(),
            "rt_wfin": nc.dram_tensor("rt_wfin", [128, NEXP], BF16, kind="ExternalOutput").ap(),
        }
    W = {
        "sg": nc.dram_tensor("sg", [D, HID0], F32, kind="ExternalInput").ap(),
        "su": nc.dram_tensor("su", [D, HID0], F32, kind="ExternalInput").ap(),
        "sd": nc.dram_tensor("sd", [HID0, D], F32, kind="ExternalInput").ap(),
        "rg": nc.dram_tensor("rg", [NEXP, D, HID0], F32, kind="ExternalInput").ap(),
        "ru": nc.dram_tensor("ru", [NEXP, D, HID0], F32, kind="ExternalInput").ap(),
        "rd": nc.dram_tensor("rd", [NEXP, HID0, D], F32, kind="ExternalInput").ap(),
        "wr": nc.dram_tensor("wr", [D, NEXP], F32, kind="ExternalInput").ap(),
        "rb": nc.dram_tensor("rb", [NEXP], F32, kind="ExternalInput").ap(),
    }
    W["wc"] = WCache(nc, "wcache", 540672 + 64)
    W["tag"] = "w0"
    with ExitStack() as stack:
        P = Prog(nc, stack)
        C = Consts(nc, P, stack, need_attn=False)
        tail = []
        emit_ffn0(nc, P, C, stack, xin, xout, W, tail, dbg=dbg)
        with nc.Block("fin") as blk:
            @blk.sync
            def _(eng):
                for sem, cnt in tail:
                    eng.wait_ge(sem.h, cnt)
    return nc


TG = 512


def emit_ffn1(nc, P, C, stack, xin, xout, W, tail, dbg=None):
    S = P.s
    ut = _tag()
    sb = lambda n, shp, dt: stack.enter_context(nc.sbuf_tensor(n + ut, shp, dt))
    x_g = sb("f1_x", [128, KC * TG], F32)
    xn_g = sb("f1_xn", [128, KC * TG], BF16)
    h_a = sb("f1_ha", [128, HC1 * TG], BF16)
    h_b = sb("f1_hb", [128, HC1 * TG], BF16)
    w1_buf = [sb(f"f1_w1{i}", [128, KC * 512], BF16) for i in range(2)]
    w2_buf = [sb(f"f1_w2{i}", [128, KC * 512], BF16) for i in range(2)]
    hh_buf = [sb(f"f1_hh{i}", [128, HC1 * 256], BF16) for i in range(2)]
    w3_buf = [sb(f"f1_w3{i}", [128, HC1 * 128], BF16) for i in range(2)]
    t1_buf = [sb(f"f1_t1{i}", [128, 512], BF16) for i in range(2)]
    sc = {n: sb("f1_s_" + n, [128, HC1], F32) for n in ("s1", "s2", "sh1", "sh2")}
    s3h = sb("f1_s3h", [128, KC], F32)
    s3t = sb("f1_s3t", [128, KC], F32)

    ps = lambda n: stack.enter_context(nc.psum_tensor(n + ut, [128, 512], F32))
    pp = [ps(f"f1_p{i}") for i in range(6)]
    lnp = make_lnp(nc, stack, "f1" + ut, TG, xn_g[:, 0:KC * TG],
                   {"s1": pp[4][0:1, 0:512], "s2": pp[5][0:1, 0:512],
                    "mb": pp[0][:, 0:512], "rb": pp[1][:, 0:512]})
    lnp["sq"] = h_a[:, 0:KC * TG]

    c_s = 0
    for n in ("s1", "s2", "sh1", "sh2"):
        for hc in range(HC1):
            c_s = P.dma("sync", S["dx"], sc[n][:, hc:hc + 1],
                        W[n].rearrange("(a b) -> a b", b=1)[
                            hc * 128:(hc + 1) * 128, :])
    c_s3 = 0
    for oc in range(KC):
        c_s3 = P.dma("sync", S["dx"], s3t[:, oc:oc + 1],
                     W["s3"].rearrange("(a b) -> a b", b=1)[
                         oc * 128:(oc + 1) * 128, :])
    c_s3h = P.op("vector", S["dve"], [(S["dx"], c_s3)],
                 lambda e: e.tensor_scalar(s3h[:], s3t[:], 0.5, None, ALU.mult))

    wc = W["wc"]
    tag = W["tag"]
    slot_x_1 = [[], []]; slot_x_2 = [[], []]; slot_x_h = [[], []]; slot_x_3 = [[], []]
    cons_w1 = [0, 0]; cons_w2 = [0, 0]; cons_hh = [0, 0]; cons_w3 = [0, 0]
    c_g_done = 0
    c_last_silu = 0
    c_last_stt = 0
    c_wb_store = 0
    c_ln_var = 0
    NCH12 = 8
    NCHH = 16
    NCH3 = 8

    for g in range(T // TG):
        gsl = slice(g * TG, (g + 1) * TG)
        c_x = P.dma("sync", S["dx"], x_g[:].rearrange("p (c t) -> p c t", t=TG),
                    fm3(xin)[:, :, gsl],
                    waits=[(S["dve"], c_g_done), (S["dx"], c_wb_store)])
        ln = emit_ln(P, C, lnp, x_g, xn_g, TG, (S["dx"], c_x),
                     pe_stat_waits=[(S["dve"], c_g_done), (S["dve"], c_ln_var)],
                     pe_bc_waits=[(S["act"], c_last_silu)])
        c_xn = ln["xn"]
        c_ln_var = ln["var"]

        prod_1 = {}
        prod_2 = {}

        def sched_12(ci):
            slot = ci % 2
            prod_1[ci], x1 = wc.load(
                P, (tag, "w1", ci),
                w1_buf[slot][:].rearrange("p (c o) -> p c o", o=512),
                w1_buf[slot][:],
                W["w1"].rearrange("(c p) o -> p c o", p=128)[
                    :, :, ci * 512:(ci + 1) * 512],
                KC * 512,
                [(S["pe"], cons_w1[slot])] + slot_x_1[slot])
            slot_x_1[slot] = [x1] if x1 else []
            prod_2[ci], x2 = wc.load(
                P, (tag, "w2", ci),
                w2_buf[slot][:].rearrange("p (c o) -> p c o", o=512),
                w2_buf[slot][:],
                W["w2"].rearrange("(c p) o -> p c o", p=128)[
                    :, :, ci * 512:(ci + 1) * 512],
                KC * 512,
                [(S["pe"], cons_w2[slot])] + slot_x_2[slot])
            slot_x_2[slot] = [x2] if x2 else []

        sched_12(0)
        sched_12(1)
        silu_cnt = {}
        stt_cnt = {}
        sidx = 0
        for ci in range(NCH12):
            slot = ci % 2
            for hcl in range(4):
                hc = ci * 4 + hcl
                pi = sidx % 2

                def mm1(e, slot=slot, hcl=hcl, pi=pi):
                    for c in range(KC):
                        last = e.matmul(
                            pp[pi][:, 0:512],
                            w1_buf[slot][:, c * 512 + hcl * 128: c * 512 + (hcl + 1) * 128],
                            xn_g[:, c * TG: (c + 1) * TG],
                            start=(c == 0), stop=(c == KC - 1))
                    return last
                w = [prod_1[ci], (S["dve"], c_xn)]
                prev_silu = silu_cnt.get(sidx - 2)
                if prev_silu:
                    w.append((S["act"], prev_silu))
                elif c_last_silu:
                    w.append((S["act"], c_last_silu))
                c_p1 = P.op("tensor", S["pe"], w, mm1)

                def mm2(e, slot=slot, hcl=hcl, pi=pi):
                    for c in range(KC):
                        last = e.matmul(
                            pp[2 + pi][:, 0:512],
                            w2_buf[slot][:, c * 512 + hcl * 128: c * 512 + (hcl + 1) * 128],
                            xn_g[:, c * TG: (c + 1) * TG],
                            start=(c == 0), stop=(c == KC - 1))
                    return last
                w = [prod_2[ci]]
                prev_stt = stt_cnt.get(sidx - 2)
                if prev_stt:
                    w.append((S["dve"], prev_stt))
                elif c_last_stt:
                    w.append((S["dve"], c_last_stt))
                c_p2 = P.op("tensor", S["pe"], w, mm2)
                cons_w1[slot] = c_p2
                cons_w2[slot] = c_p2

                rot = sidx % 2
                w = [(S["pe"], c_p1), (S["dx"], c_s)]
                if stt_cnt.get(sidx - 2):
                    w.append((S["dve"], stt_cnt[sidx - 2]))
                elif c_last_stt:
                    w.append((S["dve"], c_last_stt))
                silu_cnt[sidx] = P.op(
                    "scalar", S["act"], w,
                    lambda e, pi=pi, rot=rot, hc=hc: e.activation(
                        t1_buf[rot][:], pp[pi][:, 0:512], AF.Silu,
                        scale=sc["s1"][:, hc:hc + 1]))

                stt_cnt[sidx] = P.op(
                    "vector", S["dve"],
                    [(S["pe"], c_p2), (S["act"], silu_cnt[sidx])],
                    lambda e, pi=pi, rot=rot, hc=hc: e.scalar_tensor_tensor(
                        h_a[:, hc * TG:(hc + 1) * TG], pp[2 + pi][:, 0:512],
                        sc["s2"][:, hc:hc + 1], t1_buf[rot][:],
                        ALU.mult, ALU.mult))
                sidx += 1
            if ci + 2 < NCH12:
                sched_12(ci + 2)
        c_last_silu = silu_cnt[sidx - 1]
        c_last_stt = stt_cnt[sidx - 1]
        c_h_ready = stt_cnt[sidx - 1]

        for ph, (wkey, skey, src_h, dst_h) in enumerate(
                [("h1", "sh1", h_a, h_b), ("h2", "sh2", h_b, h_a)]):
            prod_h = {}

            def sched_h(ci, wkey=wkey):
                slot = ci % 2
                prod_h[ci], x1 = wc.load(
                    P, (tag, wkey, ci),
                    hh_buf[slot][:].rearrange("p (c o) -> p c o", o=256),
                    hh_buf[slot][:],
                    W[wkey].rearrange("(c p) o -> p c o", p=128)[
                        :, :, ci * 256:(ci + 1) * 256],
                    HC1 * 256,
                    [(S["pe"], cons_hh[slot])] + slot_x_h[slot])
                slot_x_h[slot] = [x1] if x1 else []

            sched_h(0)
            sched_h(1)
            gelu_cnt = {}
            hidx = 0
            for ci in range(NCHH):
                slot = ci % 2
                for ocl in range(2):
                    oc = ci * 2 + ocl
                    pi = hidx % 2

                    def mmh(e, slot=slot, ocl=ocl, pi=pi, src_h=src_h):
                        for hc in range(HC1):
                            last = e.matmul(
                                pp[pi][:, 0:512],
                                hh_buf[slot][:, hc * 256 + ocl * 128:
                                             hc * 256 + (ocl + 1) * 128],
                                src_h[:, hc * TG:(hc + 1) * TG],
                                start=(hc == 0), stop=(hc == HC1 - 1))
                        return last
                    w = [prod_h[ci]]
                    if hidx == 0:
                        w.append((S["dve"] if ph == 0 else S["act"],
                                  c_h_ready))
                    prev_gelu = gelu_cnt.get(hidx - 2)
                    if prev_gelu:
                        w.append((S["act"], prev_gelu))
                    elif hidx < 2 and c_last_silu:
                        w.append((S["act"], c_last_silu))
                    c_ph = P.op("tensor", S["pe"], w, mmh)
                    cons_hh[slot] = c_ph

                    gelu_cnt[hidx] = P.op(
                        "scalar", S["act"], [(S["pe"], c_ph)],
                        lambda e, pi=pi, oc=oc, skey=skey, dst_h=dst_h: e.activation(
                            dst_h[:, oc * TG:(oc + 1) * TG], pp[pi][:, 0:512],
                            AF.Gelu, scale=sc[skey][:, oc:oc + 1]))
                    hidx += 1
                if ci + 2 < NCHH:
                    sched_h(ci + 2)
            c_h_ready = gelu_cnt[hidx - 1]
            c_last_silu = gelu_cnt[hidx - 1]

        prod_3 = {}

        def sched_3(ci):
            slot = ci % 2
            prod_3[ci], x1 = wc.load(
                P, (tag, "w3", ci),
                w3_buf[slot][:].rearrange("p (c o) -> p c o", o=128),
                w3_buf[slot][:],
                W["w3"].rearrange("(c p) o -> p c o", p=128)[
                    :, :, ci * 128:(ci + 1) * 128],
                HC1 * 128,
                [(S["pe"], cons_w3[slot])] + slot_x_3[slot])
            slot_x_3[slot] = [x1] if x1 else []

        sched_3(0)
        sched_3(1)
        ev3_cnt = {}
        for oc in range(NCH3):
            slot = oc % 2
            pi = oc % 2

            def mm3(e, slot=slot, pi=pi):
                for hc in range(HC1):
                    last = e.matmul(
                        pp[2 + pi][:, 0:512],
                        w3_buf[slot][:, hc * 128:(hc + 1) * 128],
                        h_a[:, hc * TG:(hc + 1) * TG],
                        start=(hc == 0), stop=(hc == HC1 - 1))
                return last
            w = [prod_3[oc]]
            if oc == 0:
                w.append((S["act"], c_h_ready))
            prev_ev = ev3_cnt.get(oc - 2)
            if prev_ev:
                w.append((S["dve"], prev_ev))
            elif oc < 2 and c_last_stt:
                w.append((S["dve"], c_last_stt))
            c_p3 = P.op("tensor", S["pe"], w, mm3)
            cons_w3[slot] = c_p3

            ev3_cnt[oc] = P.op(
                "vector", S["dve"],
                [(S["pe"], c_p3), (S["dve"], c_s3h)],
                lambda e, oc=oc, pi=pi: e.scalar_tensor_tensor(
                    x_g[:, oc * TG:(oc + 1) * TG], pp[2 + pi][:, 0:512],
                    s3h[:, oc:oc + 1],
                    x_g[:, oc * TG:(oc + 1) * TG], ALU.mult, ALU.add))
            if oc + 2 < NCH3:
                sched_3(oc + 2)
        c_last_stt = ev3_cnt[NCH3 - 1]
        c_g_done = ev3_cnt[NCH3 - 1]
        c_wb_store = P.dma("sync", S["dx"],
                           fm3(xout)[:, :, gsl],
                           x_g[:].rearrange("p (c t) -> p c t", t=TG),
                           waits=[(S["dve"], c_g_done)])
        if dbg is not None and g == 0:
            tail.append((S["dq"], P.dma(
                "scalar", S["dq"], dbg["h"].rearrange("(c p) t -> p c t", p=128),
                h_b[:].rearrange("p (c t) -> p c t", t=TG),
                waits=[(S["act"], c_h_ready)])))
    tail.append((S["dx"], c_wb_store))
    P.flush("ffn1" + ut)


def build_ffn1_program(debug=False):
    nc = bass.Bass("TRN2", target_bir_lowering=False, debug=False)
    xin = nc.dram_tensor("x", [D, T], F32, kind="ExternalInput").ap()
    xout = nc.dram_tensor("y", [D, T], F32, kind="ExternalOutput").ap()
    W = {
        "w1": nc.dram_tensor("w1", [D, HID1], F32, kind="ExternalInput").ap(),
        "w2": nc.dram_tensor("w2", [D, HID1], F32, kind="ExternalInput").ap(),
        "h1": nc.dram_tensor("h1", [HID1, HID1], F32, kind="ExternalInput").ap(),
        "h2": nc.dram_tensor("h2", [HID1, HID1], F32, kind="ExternalInput").ap(),
        "w3": nc.dram_tensor("w3", [HID1, D], F32, kind="ExternalInput").ap(),
        "s1": nc.dram_tensor("s1", [HID1], F32, kind="ExternalInput").ap(),
        "s2": nc.dram_tensor("s2", [HID1], F32, kind="ExternalInput").ap(),
        "sh1": nc.dram_tensor("sh1", [HID1], F32, kind="ExternalInput").ap(),
        "sh2": nc.dram_tensor("sh2", [HID1], F32, kind="ExternalInput").ap(),
        "s3": nc.dram_tensor("s3", [D], F32, kind="ExternalInput").ap(),
    }
    dbg = None
    if debug:
        dbg = {"h": nc.dram_tensor("h_dump", [HID1, TG], BF16,
                                   kind="ExternalOutput").ap()}
    W["wc"] = WCache(nc, "wcache", 360448 + 64)
    W["tag"] = "w1"
    with ExitStack() as stack:
        P = Prog(nc, stack)
        C = Consts(nc, P, stack, need_attn=False)
        tail = []
        emit_ffn1(nc, P, C, stack, xin, xout, W, tail, dbg=dbg)
        with nc.Block("fin") as blk:
            @blk.sync
            def _(eng):
                for sem, cnt in tail:
                    eng.wait_ge(sem.h, cnt)
    return nc



def emit_attn(nc, P, C, stack, xin, xout, W, tail, dbg=None):
    S = P.s
    ut = _tag()
    sb = lambda n, shp, dt: stack.enter_context(nc.sbuf_tensor(n + ut, shp, dt))
    x_t = sb("at_x", [128, KC * 256], F32)
    xn_t = [sb(f"at_xn{i}", [128, KC * 512], BF16) for i in range(2)]
    kf = sb("at_kf", [128, KC * T], BF16)
    vtok = sb("at_vtok", [128, 32 * (NH * 65)], BF16)
    wq_buf = [sb(f"at_wq{i}", [128, KC * 512], BF16) for i in range(2)]
    k_ext = sb("at_kext", [128, T], BF16)
    q_ext = sb("at_qext", [128, T], BF16)
    p_sb = [sb(f"at_pp{i}", [128, 512], BF16) for i in range(6)]
    psb3 = p_sb[3]
    qstage = [p_sb[0][:, 0:512], p_sb[1][:, 0:512]]
    vstage = p_sb[2][:, 0:512]
    att_sb = [sb(f"at_att{i}", [64, 512], BF16) for i in range(2)]
    lf32_0 = sb("at_lf32", [1, 512], F32)
    linv_bf = sb("at_linv", [1, 512], BF16)
    linvb_sb = sb("at_linvb", [64, 512], BF16)
    m_f32 = sb("at_m", [128, 16], F32)
    mcols = sb("at_mcols", [128, 8], F32)
    mtmp = sb("at_mtmp", [128, 1], F32)
    mneg = sb("at_mneg", [128, 16], BF16)
    bias = {}
    if W["has_bias"]:
        for n in ("bq8", "bk", "bv", "bo"):
            bias[n] = sb("at_" + n, [128, KC], F32)
    ps = lambda n: stack.enter_context(nc.psum_tensor(n + ut, [128, 512], F32))
    pp = [ps(f"at_ps{i}") for i in range(5)]
    lf32 = [lf32_0, None]
    lf32[1] = x_t[0:1, 0:512]
    linv32 = x_t[64:65, 0:512]
    pst = [ps(f"at_pst{i}") for i in range(3)]
    pvtr = pp[4]
    lnp = make_lnp(nc, stack, "at" + ut, 512, k_ext[:, 0:KC * 512],
                   {"s1": pst[0][0:1, 0:512], "s2": pst[1][0:1, 0:512],
                    "mb": pst[0][:, 0:512], "rb": pst[1][:, 0:512]},
                   mb_alias=q_ext[:, 0:512], rb_alias=q_ext[:, 512:1024],
                   nm_alias=psb3[0:1, 0:512], r_alias=att_sb[0][0:1, 0:512],
                   m32_alias=x_t[0:1, 512:1024], var_alias=x_t[0:1, 1024:1536])

    c_bias = 0
    c_bias_dve = 0
    if W["has_bias"]:
        for n in ("bq8", "bk", "bv", "bo"):
            for ocb in range(KC):
                c_bias = P.dma("sync", S["dx"], bias[n][:, ocb:ocb + 1],
                               W[n].rearrange("(a b) -> a b", b=1)[
                                   ocb * 128:(ocb + 1) * 128, :])
        c_bias_dve = P.op("vector", S["dve"], [(S["dx"], c_bias)],
                          lambda e: e.tensor_scalar(bias["bq8"][:], bias["bq8"][:],
                                                    0.125, None, ALU.mult))
    c_vones = P.op("vector", S["dve"], [], lambda e: e.memset(vtok[:], 1.0))

    wc = W["wc"]
    tag = W["tag"]
    slot_x_q = [[], []]
    NOC = 3 * KC
    NCHQ = NOC // 4
    cons_wq = [0, 0]
    prod_q = {}

    def sched_q_dma(ci):
        slot = ci % 2
        ap2d, col0 = W["qkv_src"](ci * 4)
        prod_q[ci], x1 = wc.load(
            P, (tag, "qkv", ci),
            wq_buf[slot][:].rearrange("p (c o) -> p c o", o=512),
            wq_buf[slot][:],
            ap2d.rearrange("(c p) o -> p c o", p=128)[:, :, col0:col0 + 512],
            KC * 512,
            [(S["pe"], cons_wq[slot])] + slot_x_q[slot])
        slot_x_q[slot] = [x1] if x1 else []

    c_ln_var = 0
    c_ln_bc = 0
    c_mm_last = [0, 0]
    c_qst_dma = [0, 0]
    c_vtr_prev = 0
    c_vcp_prev = c_vones
    c_ev_hist = {}
    for tt in range(T // 512):
        tsl = slice(tt * 512, (tt + 1) * 512)
        xb = tt % 2
        c_x = P.dma("gpsimd", S["dw"],
                    xn_t[xb][:].rearrange("p (c t) -> p c t", t=512),
                    fm3(xin)[:, :, tsl], waits=[(S["pe"], c_mm_last[xb])])
        ln = emit_ln(P, C, lnp, None, xn_t[xb], 512, (S["dw"], c_x),
                     pe_stat_waits=[(S["dve"], c_ln_var)],
                     pe_bc_waits=[(S["act"], c_ln_bc)],
                     pre_cast=True)
        c_xn = ln["xn"]
        c_ln_var = ln["var"]
        c_ln_bc = ln["act_bc"]
        for ci in range(2):
            sched_q_dma(ci)
        for oc in range(NOC):
            ci = oc // 4
            slot = ci % 2
            pi = oc % 2

            def qkv_mm(e, oc=oc, slot=slot, pi=pi, xb=xb):
                for c in range(KC):
                    last = e.matmul(
                        pp[pi][:, 0:512],
                        wq_buf[slot][:, c * 512 + (oc % 4) * 128:
                                     c * 512 + (oc % 4 + 1) * 128],
                        xn_t[xb][:, c * 512:(c + 1) * 512],
                        start=(c == 0), stop=(c == KC - 1))
                return last
            w = [prod_q[ci], (S["dve"], c_xn)]
            prev_ev = c_ev_hist.get((tt, oc - 2)) or c_ev_hist.get(
                (tt - 1, oc + NOC - 2))
            if prev_ev:
                w.append(prev_ev)
            c_mm = P.op("tensor", S["pe"], w, qkv_mm)
            c_mm_last[xb] = c_mm
            cons_wq[slot] = c_mm
            if oc % 4 == 3 and ci + 2 < NCHQ:
                sched_q_dma(ci + 2)

            if oc < KC:
                qi = oc % 2

                def q_ev(e, pi=pi, qi=qi, oc=oc):
                    if W["has_bias"]:
                        return e.tensor_scalar(qstage[qi][:], pp[pi][:, 0:512],
                                               0.125, bias["bq8"][:, oc:oc + 1],
                                               ALU.mult, ALU.add)
                    return e.tensor_scalar(qstage[qi][:], pp[pi][:, 0:512],
                                           0.125, None, ALU.mult)
                c_ev = P.op("vector", S["dve"],
                            [(S["pe"], c_mm), (S["dx"], c_qst_dma[qi]),
                             (S["dve"], c_bias_dve)], q_ev)
                c_ev_hist[(tt, oc)] = (S["dve"], c_ev)
                c_qst_dma[qi] = P.dma(
                    "sync", S["dx"],
                    W["qf"].rearrange("(c p) t -> p c t", p=128)[:, oc:oc + 1, tsl],
                    qstage[qi][:].rearrange("p (c t) -> p c t", c=1),
                    waits=[(S["dve"], c_ev)])
            elif oc < 2 * KC:
                kc_out = oc - KC

                def k_ev(e, pi=pi, kc_out=kc_out, tt=tt):
                    dst = kf[:, kc_out * T + tt * 512: kc_out * T + (tt + 1) * 512]
                    if W["has_bias"]:
                        return e.tensor_scalar(dst, pp[pi][:, 0:512],
                                               bias["bk"][:, kc_out:kc_out + 1],
                                               None, ALU.add)
                    return e.tensor_copy(dst, pp[pi][:, 0:512])
                c_ev = P.op("vector", S["dve"], [(S["pe"], c_mm)], k_ev)
                c_ev_hist[(tt, oc)] = (S["dve"], c_ev)
            else:
                vc = oc - 2 * KC

                def v_ev(e, pi=pi, vc=vc):
                    if W["has_bias"]:
                        return e.tensor_scalar(vstage[:], pp[pi][:, 0:512],
                                               bias["bv"][:, vc:vc + 1],
                                               None, ALU.add)
                    return e.tensor_copy(vstage[:], pp[pi][:, 0:512])
                c_ev = P.op("vector", S["dve"],
                            [(S["pe"], c_mm), (S["pe"], c_vtr_prev)], v_ev)

                def v_tr(e):
                    for tb in range(4):
                        last = e.matmul(pvtr[:, tb * 128:(tb + 1) * 128],
                                        vstage[:, tb * 128:(tb + 1) * 128],
                                        C.ident[:], start=True, stop=True)
                    return last
                c_tr = P.op("tensor", S["pe"],
                            [(S["dve"], c_ev), (S["dve"], c_vcp_prev)], v_tr)
                c_vtr_prev = c_tr

                def v_cp(e, vc=vc, tt=tt):
                    for tb in range(4):
                        base = (tt * 4 + tb) * (NH * 65)
                        e.tensor_copy(
                            vtok[:, base + 2 * vc * 65: base + 2 * vc * 65 + 64],
                            pvtr[:, tb * 128: tb * 128 + 64])
                        last = e.tensor_copy(
                            vtok[:, base + (2 * vc + 1) * 65:
                                 base + (2 * vc + 1) * 65 + 64],
                            pvtr[:, tb * 128 + 64: (tb + 1) * 128])
                    return last
                c_vcp = P.op("vector", S["dve"],
                             [(S["pe"], c_tr), (S["dve"], c_vones)], v_cp)
                c_vcp_prev = c_vcp
                c_ev_hist[(tt, oc)] = (S["pe"], c_tr)
    c_phaseA_done_dve = c_vcp_prev
    P.flush("attn_A" + ut)

    def ham_warm(e):
        for i in range(20):
            last = e.matmul(pp[0][:, 0:512], C.ident[:], kf[:, 0:512],
                            start=True, stop=True)
        return last
    P.op("tensor", S["pe"], [], ham_warm)
    c_kones = P.op("vector", S["dve"], [], lambda e: (
        e.memset(k_ext[64:65, :], 1.0),
        e.memset(xn_t[0][64:65, :], 1.0))[-1])
    ke = [k_ext, xn_t[0]]
    qe = [q_ext, xn_t[1]]
    p_sb6 = [p_sb[i][:, 0:512] for i in range(6)]
    c_st_last = 0
    c_mrow_cp = 0
    c_lb_cp = 0
    c_orel = [0, 0]
    c_nmult = [0, 0]
    c_att_dma = [0, 0]
    red_hist = {}
    exp_hist = {}
    pv_hist = {}
    ridx_ctr = [0]
    NG = T // 512
    pend_tail = [None]
    for j in range(NH // 2):
        cc_kq = []
        for hp in range(2):
            h = 2 * j + hp
            c_kc = P.op("vector", S["dve"],
                        [(S["pe"], c_st_last), (S["dve"], c_kones)],
                        lambda e, h=h, hp=hp: e.tensor_copy(
                            ke[hp][0:64, :],
                            kf[(h % 2) * 64:(h % 2) * 64 + 64,
                               (h // 2) * T:(h // 2 + 1) * T]))
            c_ql = P.dma("sync", S["dx"],
                         qe[hp][0:64, :].rearrange("p (c t) -> p c t", c=1),
                         W["qf"].rearrange("(x p) t -> p x t", p=64)[:, h:h + 1, :],
                         waits=[(S["pe"], c_st_last)])
            cc_kq.append((c_kc, c_ql))

        p1_done = {}

        p1_thunks = []

        def p1_unit(g, hp, qt4, kb, j=j):
            nonlocal c_st_last
            qb = g * 512
            mb8 = 8 * hp + 4 * (g % 2)
            c_kc, c_ql = cc_kq[hp]
            ri = ridx_ctr[0]
            ridx_ctr[0] += 1
            pi = ri % 2

            def s_mm(e, qt4=qt4, kb=kb, pi=pi, qb=qb, hp=hp):
                return e.matmul(
                    pp[pi][:, 0:512],
                    qe[hp][0:64, qb + qt4 * 128: qb + (qt4 + 1) * 128],
                    ke[hp][0:64, kb * 512:(kb + 1) * 512],
                    start=True, stop=True)
            w = []
            if g == 0 and qt4 == 0 and kb == 0:
                w += [(S["dx"], c_ql), (S["dve"], c_kc)]
            prev = red_hist.get(ri - 2)
            if prev:
                w.append((S["dve"], prev))
            c_s = P.op("tensor", S["pe"], w, s_mm)
            c_st_last = c_s

            def red(e, qt4=qt4, kb=kb, pi=pi, mb8=mb8, g=g):
                if g == 0:
                    return e.tensor_reduce(m_f32[:, mb8 + qt4:mb8 + qt4 + 1],
                                           pp[pi][:, 0:512], AX.X, ALU.max)
                last = e.tensor_reduce(mcols[:, kb:kb + 1], pp[pi][:, 0:512],
                                       AX.X, ALU.max)
                if kb == g:
                    last = e.tensor_reduce(
                        m_f32[:, mb8 + qt4:mb8 + qt4 + 1],
                        mcols[:, 0:g + 1], AX.X, ALU.max)
                return last
            red_hist[ri] = P.op("vector", S["dve"], [(S["pe"], c_s)], red)
            p1_done[(g, hp)] = red_hist[ri]

        def emit_pass1(g, hp, defer=False):
            units = [(g, hp, qt4, kb) for qt4 in range(4) for kb in range(g + 1)]
            if defer:
                p1_thunks.extend(units)
            else:
                for u in units:
                    p1_unit(*u)

        def drain_p1(n):
            for _ in range(min(n, len(p1_thunks))):
                p1_unit(*p1_thunks.pop(0))

        def emit_mrow(g, hp, j=j):
            nonlocal c_mrow_cp
            qb = g * 512
            mb8 = 8 * hp + 4 * (g % 2)
            c_mneg = P.op("vector", S["dve"], [(S["dve"], p1_done[(g, hp)])],
                          lambda e, mb8=mb8: e.tensor_scalar(
                              mneg[:, mb8:mb8 + 4], m_f32[:, mb8:mb8 + 4],
                              -1.0, None, ALU.mult))

            def mrow_mms(e, mb8=mb8):
                for qt4 in range(4):
                    last = e.matmul(pp[4][0:1, qt4 * 128:(qt4 + 1) * 128],
                                    mneg[:, mb8 + qt4:mb8 + qt4 + 1], C.ident[:],
                                    start=True, stop=True)
                return last
            c_mrow = P.op("tensor", S["pe"],
                          [(S["dve"], c_mneg), (S["act"], c_mrow_cp),
                           (S["act"], c_lb_cp)], mrow_mms)
            c_mrow_cp = P.op("scalar", S["act"], [(S["pe"], c_mrow)],
                             lambda e, qb=qb, hp=hp: e.activation(
                                 qe[hp][64:65, qb:qb + 512], pp[4][0:1, 0:512],
                                 AF.Copy))
            mrow_done[(g, hp)] = c_mrow_cp
            mrow_done["latest"] = c_mrow_cp

        def release_O(g, hp):
            if g < 0:
                return
            nkb = 4 * (g + 1)
            c = P.op("scalar", S["act"],
                     [(S["pe"], pv_hist[(hp, g, nkb - 1)]),
                      (S["dx"], c_att_dma[hp])],
                     lambda e, hp=hp: (
                         e.activation(att_sb[hp][:], pp[2 + hp][0:64, 0:512],
                                      AF.Copy),
                         e.activation(lf32[hp][0:1, :], pp[2 + hp][64:65, 0:512],
                                      AF.Copy))[-1])
            c_orel[hp] = c

        def norm_tail(g, hp, j=j):
            nonlocal c_lb_cp
            if g < 0:
                return
            h = 2 * j + hp
            qb = g * 512
            c_li = P.op("vector", S["dve"], [(S["act"], c_orel[hp])],
                        lambda e, hp=hp: (
                            e.reciprocal(linv32[0:1, :], lf32[hp][0:1, :]),
                            e.tensor_copy(linv_bf[0:1, :], linv32[0:1, :]))[-1])
            c_lb = P.op("tensor", S["pe"],
                        [(S["dve"], c_li), (S["act"], c_lb_cp),
                         (S["act"], mrow_done.get("latest", 0))],
                        lambda e: e.matmul(pp[4][0:64, 0:512],
                                           C.ones_row[0:1, 0:64], linv_bf[0:1, :],
                                           start=True, stop=True))
            c_lb_cp = P.op("scalar", S["act"],
                           [(S["pe"], c_lb), (S["dve"], c_nmult[1 - hp])],
                           lambda e: e.activation(linvb_sb[:], pp[4][0:64, 0:512],
                                                  AF.Copy))
            c_m = P.op("vector", S["dve"], [(S["act"], c_lb_cp)],
                       lambda e, hp=hp: e.tensor_tensor(
                           att_sb[hp][:], att_sb[hp][:], linvb_sb[:], ALU.mult))
            c_nmult[hp] = c_m
            c_att_dma[hp] = P.dma(
                "sync", S["dx"],
                W["af"].rearrange("(x p) t -> p x t", p=64)[:, h:h + 1, qb:qb + 512],
                att_sb[hp][:].rearrange("p (c t) -> p c t", c=1),
                waits=[(S["dve"], c_m)])

        mrow_done = {}
        st_cnt = {}
        st_slot = {}
        st_issue = [0]
        exp_by_issue = {}

        def emit_st(g, hp, kb):
            nonlocal c_st_last
            qb = g * 512
            si = st_issue[0]
            st_issue[0] += 1
            slot = si % 3
            st_slot[(hp, kb)] = (slot, si)

            def st_mm(e, kb=kb, hp=hp, qb=qb, slot=slot):
                return e.matmul(
                    pst[slot][:, 0:512],
                    ke[hp][0:65, kb * 128:(kb + 1) * 128],
                    qe[hp][0:65, qb:qb + 512],
                    start=True, stop=True)
            w = []
            if kb == 0:
                w.append((S["act"], mrow_done[(g, hp)]))
            prev_exp = exp_by_issue.get(si - 3)
            if prev_exp:
                w.append((S["act"], prev_exp))
            st_cnt[(hp, kb)] = P.op("tensor", S["pe"], w, st_mm)
            c_st_last = st_cnt[(hp, kb)]

        emit_pass1(0, 0)
        emit_pass1(0, 1)
        if pend_tail[0] is not None:
            pend_tail[0]()
            pend_tail[0] = None
        for g in range(NG):
            if g + 1 < NG:
                emit_pass1(g + 1, 0, defer=True)
                emit_pass1(g + 1, 1, defer=True)
            emit_mrow(g, 0)
            emit_mrow(g, 1)
            release_O(g - 1, 0)
            release_O(g - 1, 1)
            nkb = 4 * (g + 1)
            emit_st(g, 0, 0)
            emit_st(g, 1, 0)

            def emit_pv(kb, hp, g=g, j=j, nkb=nkb):
                pr = hp * 3 + kb % 3
                rdy = msk_or_exp[(hp, kb)]

                def pv_mm(e, kb=kb, hp=hp, pr=pr, j=j,
                          first=(kb == 0), last=(kb == nkb - 1)):
                    h = 2 * j + hp
                    return e.matmul(
                        pp[2 + hp][0:65, 0:512],
                        vtok[:, kb * (NH * 65) + h * 65:
                             kb * (NH * 65) + (h + 1) * 65],
                        p_sb6[pr][:],
                        start=first, stop=last)
                w = [(S[rdy[0]], rdy[1])]
                if kb == 0:
                    w.append((S["act"], c_orel[hp]))
                pv_hist[(hp, g, kb)] = P.op("tensor", S["pe"], w, pv_mm)

            msk_or_exp = {}
            for kb in range(nkb):
                for hp in range(2):
                    sslot, sissue = st_slot[(hp, kb)]
                    pr = hp * 3 + kb % 3

                    def exp_op(e, pr=pr, sslot=sslot):
                        return e.activation(p_sb6[pr][:], pst[sslot][:, 0:512],
                                            AF.Exp)
                    w = [(S["pe"], st_cnt[(hp, kb)])]
                    prev_pv = pv_hist.get((hp, g, kb - 3)) if kb >= 3 else \
                        pv_hist.get((hp, g - 1, "last"))
                    if prev_pv:
                        w.append((S["pe"], prev_pv))
                    c_exp = P.op("scalar", S["act"], w, exp_op)
                    exp_hist[(hp, g, kb)] = c_exp
                    exp_by_issue[sissue] = c_exp
                    rdy = ("act", c_exp)
                    r = kb - 4 * g
                    if r >= 0:
                        c_msk = P.op("vector", S["dve"], [(S["act"], c_exp)],
                                     lambda e, pr=pr, r=r: e.tensor_tensor(
                                         p_sb6[pr][:], p_sb6[pr][:],
                                         C.diag_mask[r], ALU.mult))
                        rdy = ("dve", c_msk)
                    msk_or_exp[(hp, kb)] = rdy
                for hp in range(2):
                    if kb + 1 < nkb:
                        emit_st(g, hp, kb + 1)
                for hp in range(2):
                    if kb >= 1:
                        emit_pv(kb - 1, hp)
                need = len(p1_thunks) - (nkb - 1 - kb - 1) * 2
                if need > 0:
                    drain_p1(need)
            drain_p1(len(p1_thunks))
            for hp in range(2):
                emit_pv(nkb - 1, hp)
            for hp in range(2):
                exp_hist[(hp, g, "last")] = exp_hist[(hp, g, nkb - 1)]
                pv_hist[(hp, g, "last")] = pv_hist[(hp, g, nkb - 1)]
            norm_tail(g - 1, 0)
            norm_tail(g - 1, 1)
        pvA = pv_hist[(0, NG - 1, 4 * NG - 1)]
        pvB = pv_hist[(1, NG - 1, 4 * NG - 1)]

        def pair_tail(j=j, pvA=pvA, pvB=pvB):
            pv_hist[(0, NG - 1, 4 * NG - 1)] = pvA
            pv_hist[(1, NG - 1, 4 * NG - 1)] = pvB
            release_O(NG - 1, 0)
            release_O(NG - 1, 1)
            norm_tail(NG - 1, 0, j=j)
            norm_tail(NG - 1, 1, j=j)
        pend_tail[0] = pair_tail
    pend_tail[0]()
    P.flush("attn_B" + ut)

    c_wo_pairs = []
    for half in range(2):
        pr, _xw = wc.load(
            P, (tag, "wo", half),
            wq_buf[half][:].rearrange("p (c o) -> p c o", o=D),
            wq_buf[half][:],
            W["wo"].rearrange("(c p) o -> p c o", p=128)[
                :, half * 4:(half + 1) * 4, :],
            4 * D, [])
        c_wo_pairs.append(pr)
    c_ev_prev = 0
    c_ev3 = {}
    c_mmC = {}
    c_wb_store = 0
    NT2 = 256
    for tt in range(T // NT2):
        tsl = slice(tt * NT2, (tt + 1) * NT2)
        c_x = P.dma("sync", S["dx"], x_t[:].rearrange("p (c t) -> p c t", t=NT2),
                    fm3(xin)[:, :, tsl],
                    waits=[(S["dve"], c_ev_prev), (S["dx"], c_wb_store)])
        ab = tt % 2
        c_a = P.dma("sync", S["dx"],
                    xn_t[ab][:, 0:KC * NT2].rearrange("p (c t) -> p c t", t=NT2),
                    fm3(W["af"])[:, :, tsl],
                    waits=[(S["pe"], c_mmC.get(tt - 2, 0))])
        for oc in range(KC):
            pi = oc % 2

            def wo_mm(e, oc=oc, pi=pi, ab=ab):
                for c in range(KC):
                    last = e.matmul(
                        pp[pi][:, 0:NT2],
                        wq_buf[c // 4][:, (c % 4) * D + oc * 128:
                                       (c % 4) * D + (oc + 1) * 128],
                        xn_t[ab][:, c * NT2:(c + 1) * NT2],
                        start=(c == 0), stop=(c == KC - 1))
                return last
            w = c_wo_pairs + [(S["dx"], c_a)]
            prev = c_ev3.get((tt, oc - 2)) or c_ev3.get((tt - 1, oc + KC - 2))
            if prev:
                w.append((S["dve"], prev))
            c_mm = P.op("tensor", S["pe"], w, wo_mm)
            c_mmC[tt] = c_mm

            if W["has_bias"]:
                def ev(e, oc=oc, pi=pi):
                    xsl = slice(oc * NT2, (oc + 1) * NT2)
                    e.tensor_scalar(vstage[:, 0:NT2], pp[pi][:, 0:NT2],
                                    bias["bo"][:, oc:oc + 1], None, ALU.add)
                    return e.tensor_tensor(x_t[:, xsl], x_t[:, xsl],
                                           vstage[:, 0:NT2], ALU.add)
            else:
                def ev(e, oc=oc, pi=pi):
                    xsl = slice(oc * NT2, (oc + 1) * NT2)
                    return e.scalar_tensor_tensor(x_t[:, xsl], pp[pi][:, 0:NT2],
                                                  1.0, x_t[:, xsl],
                                                  ALU.mult, ALU.add)
            c_e = P.op("vector", S["dve"], [(S["pe"], c_mm), (S["dx"], c_x)], ev)
            c_ev3[(tt, oc)] = c_e
        c_ev_prev = c_ev3[(tt, KC - 1)]
        c_wb_store = P.dma("sync", S["dx"], fm3(xout)[:, :, tsl],
                           x_t[:].rearrange("p (c t) -> p c t", t=NT2),
                           waits=[(S["dve"], c_ev_prev)])
    tail.append((S["dx"], c_wb_store))
    P.flush("attn_C" + ut)


def build_attn_program(has_bias=True, debug=False):
    nc = bass.Bass("TRN2", target_bir_lowering=False, debug=False)
    xin = nc.dram_tensor("x", [D, T], F32, kind="ExternalInput").ap()
    xout = nc.dram_tensor("y", [D, T], F32, kind="ExternalOutput").ap()
    qf_kind = "ExternalOutput" if debug else "Internal"
    W = {
        "has_bias": has_bias,
        "qf": nc.dram_tensor("qf", [D, T], BF16, kind=qf_kind).ap(),
        "af": nc.dram_tensor("af", [D, T], BF16, kind=qf_kind).ap(),
        "wo": nc.dram_tensor("wo", [D, D], F32, kind="ExternalInput").ap(),
    }
    if has_bias:
        wq = nc.dram_tensor("wq", [D, D], F32, kind="ExternalInput").ap()
        wk = nc.dram_tensor("wk", [D, D], F32, kind="ExternalInput").ap()
        wv = nc.dram_tensor("wv", [D, D], F32, kind="ExternalInput").ap()
        mats = [wq, wk, wv]
        W["qkv_src"] = lambda oc: (mats[oc // KC], (oc % KC) * 128)
        for n in ("bq8", "bk", "bv", "bo"):
            nm = {"bq8": "bq"}.get(n, n)
            W[n] = nc.dram_tensor(nm, [D], F32, kind="ExternalInput").ap()
    else:
        wqkv = nc.dram_tensor("wqkv", [D, 3 * D], F32, kind="ExternalInput").ap()
        W["qkv_src"] = lambda oc: (wqkv, oc * 128)
    W["wc"] = WCache(nc, "wcache", 32768 + 64)
    W["tag"] = "a0"
    with ExitStack() as stack:
        P = Prog(nc, stack)
        C = Consts(nc, P, stack, need_attn=True)
        tail = []
        emit_attn(nc, P, C, stack, xin, xout, W, tail, dbg=debug)
        with nc.Block("fin") as blk:
            @blk.sync
            def _(eng):
                for sem, cnt in tail:
                    eng.wait_ge(sem.h, cnt)
    return nc


def _prep_all(inputs):
    c = lambda a: np.ascontiguousarray(a, dtype=np.float32)
    w0 = {
        "sg": c(inputs["L0_ffn_shared_gate"]),
        "su": c(inputs["L0_ffn_shared_up"]),
        "sd": c(inputs["L0_ffn_shared_down"]),
        "rg": c(inputs["L0_ffn_routed_gate"]),
        "ru": c(inputs["L0_ffn_routed_up"]),
        "rd": c(inputs["L0_ffn_routed_down"]),
        "wr": c(inputs["L0_ffn_Wr"]),
        "rb": c(inputs["L0_ffn_routing_bias"]),
    }
    a0 = {
        "wq": c(inputs["L0_attn_Wq"]), "wk": c(inputs["L0_attn_Wk"]),
        "wv": c(inputs["L0_attn_Wv"]), "wo": c(inputs["L0_attn_Wo"]),
        "bq": c(inputs["L0_attn_bq"]), "bk": c(inputs["L0_attn_bk"]),
        "bv": c(inputs["L0_attn_bv"]), "bo": c(inputs["L0_attn_bo"]),
    }
    a1 = {
        "wqkv": c(inputs["L1_attn_W_qkv"].T),
        "wo": c(inputs["L1_attn_W_o"].T),
    }
    w1 = {
        "w1": c(inputs["L1_ffn_w1_w"].T), "w2": c(inputs["L1_ffn_w2_w"].T),
        "h1": c(inputs["L1_ffn_h1_w"].T), "h2": c(inputs["L1_ffn_h2_w"].T),
        "w3": c(inputs["L1_ffn_w3_w"].T),
        "s1": c(inputs["L1_ffn_w1_s"]), "s2": c(inputs["L1_ffn_w2_s"]),
        "sh1": c(inputs["L1_ffn_h1_s"]), "sh2": c(inputs["L1_ffn_h2_s"]),
        "s3": c(inputs["L1_ffn_w3_s"]),
    }
    return w0, a0, a1, w1


_PROG_CACHE = {}


def _prog(name):
    if name not in _PROG_CACHE:
        if name == "ffn0":
            _PROG_CACHE[name] = build_ffn0_program()
        elif name == "ffn1":
            _PROG_CACHE[name] = build_ffn1_program()
        elif name == "attn0":
            _PROG_CACHE[name] = build_attn_program(has_bias=True)
        elif name == "attn1":
            _PROG_CACHE[name] = build_attn_program(has_bias=False)
    return _PROG_CACHE[name]


def build_fused_program():
    nc = bass.Bass("TRN2", target_bir_lowering=False, debug=False)
    xin = nc.dram_tensor("x", [D, T], F32, kind="ExternalInput").ap()
    y = nc.dram_tensor("y", [D, T], F32, kind="ExternalOutput").ap()
    qf = nc.dram_tensor("qf", [D, T], BF16, kind="Internal").ap()
    af = nc.dram_tensor("af", [D, T], BF16, kind="Internal").ap()
    W0 = {
        "sg": nc.dram_tensor("sg", [D, HID0], F32, kind="ExternalInput").ap(),
        "su": nc.dram_tensor("su", [D, HID0], F32, kind="ExternalInput").ap(),
        "sd": nc.dram_tensor("sd", [HID0, D], F32, kind="ExternalInput").ap(),
        "rg": nc.dram_tensor("rg", [NEXP, D, HID0], F32, kind="ExternalInput").ap(),
        "ru": nc.dram_tensor("ru", [NEXP, D, HID0], F32, kind="ExternalInput").ap(),
        "rd": nc.dram_tensor("rd", [NEXP, HID0, D], F32, kind="ExternalInput").ap(),
        "wr": nc.dram_tensor("wr", [D, NEXP], F32, kind="ExternalInput").ap(),
        "rb": nc.dram_tensor("rb", [NEXP], F32, kind="ExternalInput").ap(),
    }
    wq = nc.dram_tensor("wq", [D, D], F32, kind="ExternalInput").ap()
    wk = nc.dram_tensor("wk", [D, D], F32, kind="ExternalInput").ap()
    wv = nc.dram_tensor("wv", [D, D], F32, kind="ExternalInput").ap()
    mats0 = [wq, wk, wv]
    A0 = {
        "has_bias": True, "qf": qf, "af": af,
        "qkv_src": lambda oc: (mats0[oc // KC], (oc % KC) * 128),
        "wo": nc.dram_tensor("wo0", [D, D], F32, kind="ExternalInput").ap(),
        "bq8": nc.dram_tensor("bq", [D], F32, kind="ExternalInput").ap(),
        "bk": nc.dram_tensor("bk", [D], F32, kind="ExternalInput").ap(),
        "bv": nc.dram_tensor("bv", [D], F32, kind="ExternalInput").ap(),
        "bo": nc.dram_tensor("bo", [D], F32, kind="ExternalInput").ap(),
    }
    wqkv = nc.dram_tensor("wqkv", [D, 3 * D], F32, kind="ExternalInput").ap()
    A1 = {
        "has_bias": False, "qf": qf, "af": af,
        "qkv_src": lambda oc: (wqkv, oc * 128),
        "wo": nc.dram_tensor("wo1", [D, D], F32, kind="ExternalInput").ap(),
    }
    W1 = {
        "w1": nc.dram_tensor("w1", [D, HID1], F32, kind="ExternalInput").ap(),
        "w2": nc.dram_tensor("w2", [D, HID1], F32, kind="ExternalInput").ap(),
        "h1": nc.dram_tensor("h1", [HID1, HID1], F32, kind="ExternalInput").ap(),
        "h2": nc.dram_tensor("h2", [HID1, HID1], F32, kind="ExternalInput").ap(),
        "w3": nc.dram_tensor("w3", [HID1, D], F32, kind="ExternalInput").ap(),
        "s1": nc.dram_tensor("s1", [HID1], F32, kind="ExternalInput").ap(),
        "s2": nc.dram_tensor("s2", [HID1], F32, kind="ExternalInput").ap(),
        "sh1": nc.dram_tensor("sh1", [HID1], F32, kind="ExternalInput").ap(),
        "sh2": nc.dram_tensor("sh2", [HID1], F32, kind="ExternalInput").ap(),
        "s3": nc.dram_tensor("s3", [D], F32, kind="ExternalInput").ap(),
    }
    wcache = WCache(nc, "wcache", 966656 + 256)
    for wdict, tg in ((W0, "w0"), (A0, "a0"), (A1, "a1"), (W1, "w1")):
        wdict["wc"] = wcache
        wdict["tag"] = tg
    with ExitStack() as ostack:
        P = Prog(nc, ostack)
        C = Consts(nc, P, ostack, need_attn=True)
        tail = []
        seq = [
            ("ffn0", emit_ffn0, W0, xin, y),
            ("attn0", emit_attn, A0, y, y),
            ("ffn0b", emit_ffn0, W0, y, y),
            ("ffn1", emit_ffn1, W1, y, y),
            ("attn1", emit_attn, A1, y, y),
            ("ffn1b", emit_ffn1, W1, y, y),
        ]
        for name, fn, wts, si, so in seq:
            P.wait("gpsimd", P.s["dx"], P.s["dx"].n)
            with ExitStack() as stack:
                fn(nc, P, C, stack, si, so, wts, tail)
        with nc.Block("fin") as blk:
            @blk.sync
            def _(eng):
                for sem, cnt in tail:
                    eng.wait_ge(sem.h, cnt)
    return nc


def _fused_in_map(inputs):
    w0, a0, a1, w1 = _prep_all(inputs)
    m = {}
    m.update(w0)
    m.update({"wq": a0["wq"], "wk": a0["wk"], "wv": a0["wv"], "wo0": a0["wo"],
              "bq": a0["bq"], "bk": a0["bk"], "bv": a0["bv"], "bo": a0["bo"]})
    m.update({"wqkv": a1["wqkv"], "wo1": a1["wo"]})
    m.update(w1)
    return m


def kernel(**inputs):
    ncores = 8
    x = np.asarray(inputs["L0_attn_x"], dtype=np.float32)
    wmap = _fused_in_map(inputs)
    nc = build_fused_program()
    in_maps = [dict(wmap, x=np.ascontiguousarray(x[b].T)) for b in range(ncores)]
    res = run_bass_kernel_spmd(nc, in_maps, core_ids=list(range(ncores)))
    return np.stack([np.asarray(res.results[b]["y"]).T for b in range(ncores)],
                    axis=0)


def kernel_unfused(**inputs):
    ncores = 8
    x = np.asarray(inputs["L0_attn_x"], dtype=np.float32)
    cur = [np.ascontiguousarray(x[b].T) for b in range(ncores)]
    w0, a0, a1, w1 = _prep_all(inputs)
    seq = [("ffn0", w0), ("attn0", a0), ("ffn0", w0),
           ("ffn1", w1), ("attn1", a1), ("ffn1", w1)]
    for name, wts in seq:
        nc = _prog(name)
        in_maps = [dict(wts, x=cur[b]) for b in range(ncores)]
        res = run_bass_kernel_spmd(nc, in_maps, core_ids=list(range(ncores)))
        cur = [np.asarray(res.results[b]["y"]) for b in range(ncores)]
    return np.stack([cur[b].T for b in range(ncores)], axis=0)
